# version 15
# baseline (speedup 1.0000x reference)
import contextlib
import math

import numpy as np

import concourse.bass as bass
import concourse.mybir as mybir
from concourse.bass_utils import run_bass_kernel_spmd

F32 = mybir.dt.float32
BF16 = mybir.dt.bfloat16
AL = mybir.AluOpType
AF = mybir.ActivationFunctionType
AX = mybir.AxisListType
AP = bass.AP

ENGINES = ("tensor", "vector", "scalar", "gpsimd", "sync")


class _Op:
    __slots__ = ("engine", "method", "args", "kwargs", "reads", "writes", "idx",
                 "is_dma", "deps", "marked", "cnt", "dma_sem_key", "dma_seq", "tag")

    def __init__(self, engine, method, args, kwargs, reads, writes, idx, is_dma, tag):
        self.engine = engine
        self.method = method
        self.args = args
        self.kwargs = kwargs
        self.reads = reads
        self.writes = writes
        self.idx = idx
        self.is_dma = is_dma
        self.deps = []
        self.marked = False
        self.cnt = None
        self.dma_sem_key = None
        self.dma_seq = None
        self.tag = tag


def _ap_name(x):
    return x.name if isinstance(x, AP) else None


class Prog:

    def __init__(self, nc):
        self.nc = nc
        self.ops = []
        self.last_write = {}
        self.readers = {}
        self.cpool = None

    def _rec(self, engine, method, out_aps, in_aps, args, kwargs, is_dma=False, tag="",
             extra_deps=()):
        idx = len(self.ops)
        reads = [n for n in (_ap_name(a) for a in in_aps) if n]
        writes = [n for n in (_ap_name(a) for a in out_aps) if n]
        op = _Op(engine, method, args, kwargs, reads, writes, idx, is_dma, tag)
        deps = {}
        def _add(i, kind):
            if i == idx:
                return
            prev = deps.get(i)
            rank = {"war": 0, "waw": 1, "raw": 2}
            if prev is None or rank[kind] > rank[prev]:
                deps[i] = kind
        for d in extra_deps:
            _add(d, "raw")
        for b in reads:
            if b in self.last_write:
                _add(self.last_write[b], "raw")
        for b in writes:
            if b in self.last_write:
                _add(self.last_write[b], "waw")
            for r in self.readers.get(b, ()):
                _add(r, "war")
        op.deps = sorted(deps.items())
        for b in reads:
            self.readers.setdefault(b, []).append(idx)
        for b in writes:
            self.last_write[b] = idx
            self.readers[b] = []
        self.ops.append(op)
        return op

    def op(self, engine, method, outs, ins, *args, tag="", **kwargs):
        return self._rec(engine, method, outs, ins, args, kwargs, tag=tag)

    def mm(self, out, lhsT, rhs, start=True, stop=True, tile_position=None, tag=""):
        kw = dict(start=start, stop=stop)
        if tile_position is not None:
            kw["tile_position"] = tile_position
        return self._rec("tensor", "matmul", [out], [lhsT, rhs],
                         (out, lhsT, rhs), kw, tag=tag)

    def transpose(self, out, in_, ident, tag=""):
        return self._rec("tensor", "transpose", [out], [in_, ident],
                         (out, in_, ident), {}, tag=tag)

    def act(self, out, in_, func, bias=0.0, scale=1.0, accum_out=None, tag=""):
        ins = [in_] + [a for a in (bias, scale) if isinstance(a, AP)]
        if self.cpool is not None and any(
                isinstance(a, (float, int)) for a in (bias, scale)):
            ins.append(self.cpool)
        outs = [out] + ([accum_out] if accum_out is not None else [])
        kw = dict(bias=bias, scale=scale)
        if accum_out is not None:
            kw["accum_out"] = accum_out
        return self._rec("scalar", "activation", outs, ins, (out, in_, func), kw, tag=tag)

    def tt(self, out, in0, in1, alu, eng="vector", tag=""):
        return self._rec(eng, "tensor_tensor", [out], [in0, in1], (out, in0, in1, alu), {}, tag=tag)

    def ts(self, out, in0, s1, s2, op0, op1=AL.bypass, eng="vector", accum_out=None, tag=""):
        ins = [in0] + [a for a in (s1, s2) if isinstance(a, AP)]
        outs = [out] + ([accum_out] if accum_out is not None else [])
        kw = {}
        if accum_out is not None:
            kw["accum_out"] = accum_out
        return self._rec(eng, "tensor_scalar", outs, ins, (out, in0, s1, s2, op0, op1), kw, tag=tag)

    def stt(self, out, in0, s, in1, op0, op1, eng="vector", tag=""):
        ins = [in0, in1] + ([s] if isinstance(s, AP) else [])
        return self._rec(eng, "scalar_tensor_tensor", [out], ins, (out, in0, s, in1, op0, op1), {}, tag=tag)

    def scan(self, out, data0, data1, initial, op0, op1, tag=""):
        ins = [data0, data1] + ([initial] if isinstance(initial, AP) else [])
        return self._rec("vector", "tensor_tensor_scan", [out], ins,
                         (out, data0, data1, initial, op0, op1), {}, tag=tag)

    def red(self, out, in_, axis, alu, absval=None, negate=None, tag=""):
        kw = {}
        if absval is not None:
            kw["apply_absolute_value"] = absval
        if negate is not None:
            kw["negate"] = negate
        return self._rec("vector", "tensor_reduce", [out], [in_], (out, in_, axis, alu), kw, tag=tag)

    def copy(self, out, in_, eng="vector", tag=""):
        meth = "copy" if eng == "scalar" else "tensor_copy"
        return self._rec(eng, meth, [out], [in_], (out, in_), {}, tag=tag)

    def recip(self, out, in_, tag=""):
        return self._rec("vector", "reciprocal", [out], [in_], (out, in_), {}, tag=tag)

    def shuffle(self, out, in_, mask, tag=""):
        return self._rec("vector", "stream_shuffle", [out], [in_], (out, in_, mask), {}, tag=tag)

    def memset(self, ap, val, eng="vector", tag=""):
        return self._rec(eng, "memset", [ap], [], (ap, val), {}, tag=tag)

    def fence(self):
        last = {}
        last_dma = {}
        for op in self.ops:
            last[op.engine] = op.idx
            if op.is_dma:
                last_dma[op.dma_sem_key] = op.idx
        dma_deps = list(last_dma.values())
        for e in ENGINES:
            deps = [v for k, v in last.items() if k != e] + dma_deps
            self._rec(e, "nop", [], [], (), {}, tag="fence", extra_deps=deps)

    def dma(self, out, in_, eng="sync", tag="", slow_ok=False):
        kw = {"allow_slow_non_contiguous": True} if slow_ok else {}
        op = self._rec(eng, "dma_start", [out], [in_], (out, in_), kw, is_dma=True, tag=tag)
        op.dma_sem_key = op.writes[0] if op.writes else ("dmar_" + (op.reads[0] if op.reads else "x"))
        return op

    def dma_transpose(self, out, in_, eng="sync", tag=""):
        op = self._rec(eng, "dma_start_transpose", [out], [in_], (out, in_), {}, is_dma=True, tag=tag)
        op.dma_sem_key = op.writes[0] if op.writes else ("dmar_" + (op.reads[0] if op.reads else "x"))
        return op

    def emit(self, stack):
        nc = self.nc
        dma_count = {}
        for op in self.ops:
            if op.is_dma:
                k = op.dma_sem_key
                dma_count[k] = dma_count.get(k, 0) + 1
                op.dma_seq = dma_count[k]
        SE_SYNC = ("vector", "scalar", "gpsimd")

        def _needs_sync(dep, op, kind):
            if dep.is_dma:
                return False
            if dep.engine != op.engine:
                return True
            if op.is_dma:
                return kind in ("raw", "waw")
            return dep.engine in SE_SYNC

        for op in self.ops:
            for d, kind in op.deps:
                dep = self.ops[d]
                if _needs_sync(dep, op, kind):
                    dep.marked = True
        cnt = dict.fromkeys(ENGINES, 0)
        for op in self.ops:
            if op.marked:
                cnt[op.engine] += 1
                op.cnt = cnt[op.engine]
        eng_sem = {e: stack.enter_context(self.nc.semaphore(f"es_{e}"))
                   for e in ENGINES if cnt[e] > 0}
        dma_sems = {k: stack.enter_context(self.nc.semaphore(f"ds{i}"))
                    for i, k in enumerate(dma_count)}
        watermark = {}
        all_waits = []
        for op in self.ops:
            waits = {}
            for d, kind in op.deps:
                dep = self.ops[d]
                if dep.is_dma:
                    sem = dma_sems[dep.dma_sem_key]
                    v = 16 * dep.dma_seq
                elif not _needs_sync(dep, op, kind):
                    continue
                else:
                    sem = eng_sem[dep.engine]
                    v = dep.cnt
                key = sem.name
                if waits.get(key, (None, -1))[1] < v:
                    waits[key] = (sem, v)
            pruned = []
            for key, (sem, v) in waits.items():
                wm = watermark.get((op.engine, key), -1)
                if v > wm:
                    pruned.append((sem, v))
                    watermark[(op.engine, key)] = v
            all_waits.append(pruned)

        final_waits = [(dma_sems[k], 16 * dma_count[k]) for k in dma_count]

        stats = {e: 0 for e in ENGINES}
        for op in self.ops:
            stats[op.engine] += 1

        def emit_engine(eng_name):
            eng = getattr(nc, eng_name)
            my_sem = eng_sem.get(eng_name)
            for op, waits in zip(self.ops, all_waits):
                if op.engine != eng_name:
                    continue
                for sem, v in waits:
                    eng.wait_ge(sem, v)
                try:
                    inst = getattr(eng, op.method)(*op.args, **op.kwargs)
                except Exception:
                    print("EMIT FAIL", op.engine, op.method, op.tag,
                          [type(a).__name__ for a in op.args])
                    raise
                if op.is_dma:
                    inst.then_inc(dma_sems[op.dma_sem_key], 16)
                elif op.marked:
                    inst.then_inc(my_sem, 1)
            if eng_name == "sync":
                for sem, v in final_waits:
                    eng.wait_ge(sem, v)

        with nc.Block() as blk:
            for name, dec in (("sync", blk.sync), ("tensor", blk.tensor),
                              ("vector", blk.vector), ("scalar", blk.scalar),
                              ("gpsimd", blk.gpsimd)):
                def body(_eng, _name=name):
                    emit_engine(_name)
                dec(body)
        return stats



T = 2048
D = 1024
DI = 2048
NST = 32
KT_D = 8
NT_DI = 16
TCH = 256
NCH = T // TCH
FI = 2816
NIT = FI // 128
GH, GDK, GDV = 16, 64, 128
GL = 128
GC = T // GL

CONSTS = [0.5, -0.5, -1.0, 2.0, 1.0 + 1e-6, 1e-5, 1e-6, 1.0 / 128.0, 0.1,
          1.0, 0.0]

QSTEPS = [(0.25, 0.5), (0.75, 0.5), (1.25, 0.5), (1.75, 0.5),
          (2.5, 1.0), (3.5, 1.0), (5.0, 2.0)]


def build_model(upto="full", dbg=()):
    nc = bass.Bass("TRN2", target_bir_lowering=False, debug=False)
    p = Prog(nc)

    def din(name, shape):
        return nc.dram_tensor(name, list(shape), F32, kind="ExternalInput").ap()

    def dscratch(name, shape, dt=F32):
        return nc.dram_tensor(name, list(shape), dt, kind="Internal").ap()

    xT = din("xT", (D, T))
    w_inT = din("w_inT", (D, 2 * DI))
    convw = din("convw", (128, NT_DI, 3))
    convb = din("convb", (128, NT_DI))
    alog = din("alog", (128, NT_DI, NST))
    dskin = din("dsk", (128, NT_DI))
    wxT = din("wxT", (128, NT_DI, 65))
    wdt = din("wdt", (128, NT_DI))
    bdt = din("bdt", (128, NT_DI))
    alpha_in = din("alpha", (1,))
    woutT = din("woutT", (DI, D))
    wg_in = din("wg", (FI, D))
    wu_in = din("wu", (FI, D))
    wd_in = din("wd", (D, FI))
    wqT = din("wqT", (D, D))
    wkT = din("wkT", (D, D))
    wvT = din("wvT", (D, 2048))
    wbT = din("wbT", (D, 16))
    waT = din("waT", (D, 16))
    al1 = din("al1", (16,))
    dtb1 = din("dtb1", (16,))
    woT = din("woT", (2048, D))
    cq_in = din("cq", (128, 8, 4))
    ck_in = din("ck", (128, 8, 4))
    cv_in = din("cv", (128, 16, 4))
    cw1 = din("cw1", (1024, 1024))
    cb1 = din("cb1", (128, 8))
    cw2 = din("cw2", (1024, 512))
    cb2 = din("cb2", (128, 4))
    cw3 = din("cw3", (512, 256))
    cb3 = din("cb3", (128, 2))
    cw4 = din("cw4", (256, 1))
    cb4 = din("cb4", (1,))
    gw1 = din("gw1", (1024, 512))
    gb1 = din("gb1", (128, 4))
    gw2 = din("gw2", (512, 256))
    gb2 = din("gb2", (128, 2))
    gw3 = din("gw3", (256, 1))
    gb3 = din("gb3", (1,))
    ar_in = din("ar", (1,))
    iden_in = din("iden_c", (128, 128))
    cvec = din("cvec", (len(CONSTS),))
    mb_ls = din("mb_ls", (128, 128))
    mb_us = din("mb_us", (128, 128))
    mb_ui = din("mb_ui", (128, 128))
    creset = din("creset", (16, T))

    out_hbm = nc.dram_tensor("out", [D, T], F32, kind="ExternalOutput").ap()
    dbgo = {}
    if dbg:
        for nm, shape, dt in [("d_u", (128, NT_DI, TCH), BF16), ("d_z", (128, NT_DI, TCH), BF16),
                              ("d_proj", (65, TCH), F32), ("d_dtr", (128, TCH), F32),
                              ("d_brep", (128, NST, TCH), BF16), ("d_crep", (128, NST, TCH), BF16),
                              ("d_dta", (128, NST, TCH + 1), BF16), ("d_x", (128, NST, TCH + 1), BF16),
                              ("d_h", (128, NST, TCH + 1), BF16), ("d_dt", (128, TCH), F32),
                              ("d_y", (128, TCH), F32), ("d_yg", (128, NT_DI, TCH), BF16)]:
            dbgo[nm] = nc.dram_tensor(nm, list(shape), dt, kind="ExternalOutput").ap()
    if dbg and "g" in dbg:
        for nm, shape, dt in [("g_qd", (128, 8, T), BF16), ("g_kd", (128, 8, T), BF16),
                              ("g_sg", (16, T), F32), ("g_ta", (16, T), F32),
                              ("g_P0", (128, 128), BF16), ("g_P0T", (128, 128), BF16),
                              ("g_Wd", (128, 128), BF16), ("g_obd", (128, 128), F32),
                              ("g_Sd", (128, 8, 128), BF16), ("g_vt", (128, 128), BF16),
                              ("g_ost", (128, 128), BF16)]:
            dbgo[nm] = nc.dram_tensor(nm, list(shape), dt, kind="ExternalOutput").ap()

    x1 = dscratch("x1", (D, T))
    fw = dscratch("fw", (D, T))
    h_hbm = dscratch("h_hbm", (T, FI), F32)
    xq_hbm = dscratch("xq_hbm", (16, 128, 8, 128), BF16)
    v_hbm = dscratch("v_hbm", (16, 128, T), BF16)
    q_hbm = dscratch("q_hbm", (8, 128, T), BF16)
    k_hbm = dscratch("k_hbm", (8, 128, T), BF16)
    o_hbm = dscratch("o_hbm", (2048, T), BF16)
    hb_dtr = dscratch("hb_dtr", (TCH,))
    hb_b = dscratch("hb_b", (NST * TCH,), BF16)
    hb_c = dscratch("hb_c", (NST * TCH,), BF16)
    hb_row1 = dscratch("hb_row1", (T,))
    hb_row2 = dscratch("hb_row2", (T,))
    hb_row3 = dscratch("hb_row3", (T,))
    hb_row4 = dscratch("hb_row4", (T,))
    hb_row5 = dscratch("hb_row5", (T,))
    hb_row6 = dscratch("hb_row6", (T,))
    hb_sc = dscratch("hb_sc", (8,))
    hb_se = dscratch("hb_se", (16 * 16,))
    sg_hbm = dscratch("sg_hbm", (2, 16, T))
    hb_nv = dscratch("hb_nv", (2 * T,))
    hb_cp = dscratch("hb_cp", (512,))

    _uid = [0]

    def _mk(n):
        _uid[0] += 1
        return f"{n}_{_uid[0]}"

    st = contextlib.ExitStack()
    with st:
        def sb(name, shape, dt=F32):
            return st.enter_context(nc.sbuf_tensor(name, list(shape), dt))

        cpool = sb("cpool", [128, len(CONSTS)])
        p.dma(cpool[:], cvec.unsqueeze(0).broadcast_to([128, len(CONSTS)]))
        for i, v in enumerate(CONSTS):
            nc.const_aps.register(F32, v, cpool[:, i:i + 1])
        p.cpool = cpool[:]
        ones_col = cpool[:, CONSTS.index(1.0):CONSTS.index(1.0) + 1]

        iden = sb("iden", [128, 128])
        iden_bf = sb("iden_bf", [128, 128], BF16)
        p.dma(iden[:], iden_in)
        p.copy(iden_bf[:], iden[:])

        xn = sb("xn", [128, KT_D, T], BF16)

        def tposef(out_psum, in_ap):
            p.transpose(out_psum, in_ap, iden[:])

        def tposeb(out_psum, in_ap):
            p.transpose(out_psum, in_ap, iden_bf[:])

        pb = [st.enter_context(nc.psum_tensor(f"pb{i}", [128, 512], F32))
              for i in range(8)]

        def pbb(i):
            return pb[i][:, 0:64].bitcast(BF16)

        convw_sb = sb("convw_sb", [128, NT_DI, 3])
        convb_sb = sb("convb_sb", [128, NT_DI])
        wdt_sb = sb("wdt_sb", [128, NT_DI])
        bdt_sb = sb("bdt_sb", [128, NT_DI])
        dsk_sb = sb("dsk_sb", [128, NT_DI])
        AR_sb = sb("AR_sb", [128, NT_DI, NST])
        wxT_sb = sb("wxT_sb", [128, NT_DI, 65], BF16)
        lna_bc = sb("lna_bc", [128, 1])
        l1ma_bc = sb("l1ma_bc", [128, 1])
        edge = sb("edge", [128, NT_DI])
        hcarry = sb("hcarry", [128, NT_DI, NST], BF16)

        p.dma(convw_sb[:], convw)
        p.dma(convb_sb[:], convb)
        p.dma(wdt_sb[:], wdt)
        p.dma(bdt_sb[:], bdt)
        p.dma(dsk_sb[:], dskin)
        p.dma(AR_sb[:], alog)
        p.act(AR_sb[:], AR_sb[:], AF.Exp)
        p.ts(AR_sb[:], AR_sb[:], -1.0, None, AL.mult)
        with nc.sbuf_tensor("wxf", [128, NT_DI, 65], F32) as wxf:
            p.dma(wxf[:], wxT)
            p.copy(wxT_sb[:], wxf[:])
        p.fence()
        asc = sb("asc", [1, 4])
        p.dma(asc[0:1, 0:1], alpha_in.unsqueeze(0))
        p.act(asc[0:1, 1:2], asc[0:1, 0:1], AF.Sigmoid)
        p.act(asc[0:1, 2:3], asc[0:1, 1:2], AF.Ln)
        p.act(asc[0:1, 3:4], asc[0:1, 1:2], AF.Ln, scale=-1.0, bias=1.0)
        p.dma(hb_sc[0:1], asc[0:1, 2:3])
        p.dma(hb_sc[1:2], asc[0:1, 3:4])
        p.dma(lna_bc[:], hb_sc[0:1].unsqueeze(0).broadcast_to([128, 1]))
        p.dma(l1ma_bc[:], hb_sc[1:2].unsqueeze(0).broadcast_to([128, 1]))

        C1P = 1.0 + 1e-6

        def layer_norm(src, write_fw):
            with contextlib.ExitStack() as ph:
                def ar(name, shape, dt=F32):
                    return ph.enter_context(nc.sbuf_tensor(_mk(name), list(shape), dt))
                xld = ar("ln_x", [128, T])
                x2t = ar("ln_x2", [128, T])
                srow = ar("ln_s", [1, T])
                qrow = ar("ln_q", [1, T])
                abc = ar("ln_abc", [128, T])
                bbc = ar("ln_bbc", [128, T])
                xnf = ar("ln_xnf", [128, T])
                for kt in range(KT_D):
                    p.dma(xld[:], src[kt * 128:(kt + 1) * 128, :], tag="lnx")
                    p.act(x2t[:], xld[:], AF.Square)
                    for nn in range(4):
                        sl = slice(nn * 512, (nn + 1) * 512)
                        p.mm(pb[nn][0:1, 0:512], ones_col, xld[:, sl],
                             start=(kt == 0), stop=(kt == KT_D - 1))
                        p.mm(pb[4 + nn][0:1, 0:512], ones_col, x2t[:, sl],
                             start=(kt == 0), stop=(kt == KT_D - 1))
                for nn in range(4):
                    sl = slice(nn * 512, (nn + 1) * 512)
                    p.copy(srow[0:1, sl], pb[nn][0:1, 0:512])
                    p.copy(qrow[0:1, sl], pb[4 + nn][0:1, 0:512], eng="scalar")
                p.ts(srow[0:1, :], srow[0:1, :], 1.0 / D, None, AL.mult)
                p.ts(qrow[0:1, :], qrow[0:1, :], 1.0 / D, None, AL.mult)
                mu2 = ar("ln_m2", [1, T])
                p.tt(mu2[0:1, :], srow[0:1, :], srow[0:1, :], AL.mult)
                p.tt(qrow[0:1, :], qrow[0:1, :], mu2[0:1, :], AL.subtract)
                p.act(qrow[0:1, :], qrow[0:1, :], AF.Ln, bias=1e-5)
                p.act(qrow[0:1, :], qrow[0:1, :], AF.Exp, scale=-0.5)
                p.stt(mu2[0:1, :], srow[0:1, :], -1.0, qrow[0:1, :], AL.mult, AL.mult)
                p.dma(hb_row1.unsqueeze(0), qrow[0:1, :])
                p.dma(hb_row2.unsqueeze(0), mu2[0:1, :])
                p.dma(abc[:], hb_row1.unsqueeze(0).broadcast_to([128, T]))
                p.dma(bbc[:], hb_row2.unsqueeze(0).broadcast_to([128, T]))
                for kt in range(KT_D):
                    p.dma(xld[:], src[kt * 128:(kt + 1) * 128, :], tag="lnx2")
                    p.tt(xnf[:], xld[:], abc[:], AL.mult)
                    p.tt(xnf[:], xnf[:], bbc[:], AL.add)
                    p.copy(xn[:, kt, :], xnf[:])
                    if write_fw:
                        p.dma(fw[kt * 128:(kt + 1) * 128, :], xnf[:], tag="fwst")
            p.fence()

        def mamba():
            with contextlib.ExitStack() as ph:
                def ar(name, shape, dt=F32):
                    return ph.enter_context(nc.sbuf_tensor(_mk(name), list(shape), dt))
                dtraw_bc = ar("m_dtraw", [128, TCH])
                Brep = ar("m_brep", [128, NST, TCH], BF16)
                Crep = ar("m_crep", [128, NST, TCH], BF16)
                uch = ar("m_uch", [128, NT_DI, TCH], BF16)
                zch = ar("m_zch", [128, NT_DI, TCH], BF16)
                ygch = ar("m_ygch", [128, NT_DI, TCH], BF16)
                dtaR = [ar("m_dta", [128, NST, TCH + 1], BF16) for _ in range(2)]
                XbR = [ar("m_xb", [128, NST, TCH + 1], BF16) for _ in range(2)]
                hbufR = [ar("m_hbuf", [128, NST, TCH + 1], BF16) for _ in range(2)]
                x_inR = [ar("m_xin", [128, TCH + 2])] * 2
                dt_tR = [ar("m_dt", [128, TCH]) for _ in range(2)]
                dt_sR = [ar("m_dts", [128, TCH])] * 2
                du_tR = [ar("m_du", [128, TCH], BF16) for _ in range(2)]
                y_tR = [ar("m_yt", [128, TCH])] * 2
                proj_sb = ar("m_proj", [128, TCH])

                wstgbR = [ar("m_wstgb", [128, 16, 128], BF16) for _ in range(2)]
                xsbR = [ar("m_xsb", [128, TCH])] * 2
                Bbf = xsbR[0][:].bitcast(BF16)[:, 0:TCH]
                p.memset(dtaR[0][:, :, 0:1], 0.0)
                p.memset(dtaR[1][:, :, 0:1], 0.0)
                p.memset(edge[:], 0.0)

                tails = []

                def _finish_unit(dta, y_t, dtile):
                    hc = dta[:, :, 1:TCH + 1]
                    p.tt(hc[:, 0:16, :], hc[:, 0:16, :], hc[:, 16:32, :], AL.add)
                    p.tt(hc[:, 0:8, :], hc[:, 0:8, :], hc[:, 8:16, :], AL.add)
                    p.tt(hc[:, 0:4, :], hc[:, 0:4, :], hc[:, 4:8, :], AL.add)
                    p.tt(hc[:, 0:2, :], hc[:, 0:2, :], hc[:, 2:4, :], AL.add)
                    p.tt(y_t[:], hc[:, 0, :], hc[:, 1, :], AL.add)
                    p.stt(y_t[:], uch[:, dtile, :], dsk_sb[:, dtile:dtile + 1],
                          y_t[:], AL.mult, AL.add)
                    p.tt(ygch[:, dtile, :], y_t[:], zch[:, dtile, :], AL.mult)

                def phase_d(tcd):
                    t0d = tcd * TCH
                    for i in range(KT_D):
                        bank = pb[3 + (i % 2)]
                        wstgb = wstgbR[i % 2]
                        xsb = xsbR[i % 2]
                        p.dma(wstgb[:],
                              woutT.rearrange("(a p) c -> p a c", p=128)[:, :, i * 128:(i + 1) * 128],
                              tag="wout", eng="gpsimd")
                        for kt in range(NT_DI):
                            p.mm(bank[:, 0:TCH], wstgb[:, kt, :], ygch[:, kt, :],
                                 start=(kt == 0), stop=(kt == NT_DI - 1))
                        p.dma(xsb[:], xT[i * 128:(i + 1) * 128, t0d:t0d + TCH], tag="xres")
                        p.stt(xsb[:], bank[:, 0:TCH], 1.0, xsb[:], AL.mult, AL.add)
                        p.dma(x1[i * 128:(i + 1) * 128, t0d:t0d + TCH], xsb[:], tag="x1st")

                for tc in range(NCH):
                    t0 = tc * TCH
                    nx = min(TCH + 1, T - t0)
                    for dtile in range(NT_DI):
                        wstgb = wstgbR[dtile % 2]
                        x_in = x_inR[dtile % 2]
                        dt_s = dt_sR[dtile % 2]
                        c0 = dtile * 128
                        p.dma(wstgb[:, 0:8, :],
                              w_inT.rearrange("(a p) c -> p a c", p=128)[:, :, c0:c0 + 128],
                              tag="wina", eng="gpsimd")
                        for kt in range(KT_D):
                            p.mm(pb[0][:, 0:nx], wstgb[:, kt, :], xn[:, kt, t0:t0 + nx],
                                 start=(kt == 0), stop=(kt == KT_D - 1))
                        zc0 = DI + dtile * 128
                        p.dma(wstgb[:, 8:16, :],
                              w_inT.rearrange("(a p) c -> p a c", p=128)[:, :, zc0:zc0 + 128],
                              tag="winb", eng="gpsimd")
                        for kt in range(KT_D):
                            p.mm(pb[1][:, 0:TCH], wstgb[:, 8 + kt, :], xn[:, kt, t0:t0 + TCH],
                                 start=(kt == 0), stop=(kt == KT_D - 1))
                        p.copy(x_in[:, 0:1], edge[:, dtile:dtile + 1])
                        p.copy(x_in[:, 1:1 + nx], pb[0][:, 0:nx])
                        if nx < TCH + 1:
                            p.memset(x_in[:, 1 + nx:TCH + 2], 0.0)
                        p.copy(edge[:, dtile:dtile + 1], x_in[:, TCH:TCH + 1])
                        p.ts(dt_s[:], x_in[:, 0:TCH], convw_sb[:, dtile, 0:1], None, AL.mult)
                        p.stt(dt_s[:], x_in[:, 1:TCH + 1], convw_sb[:, dtile, 1:2], dt_s[:],
                              AL.mult, AL.add)
                        p.stt(dt_s[:], x_in[:, 2:TCH + 2], convw_sb[:, dtile, 2:3], dt_s[:],
                              AL.mult, AL.add)
                        p.act(uch[:, dtile, :], dt_s[:], AF.Silu,
                              bias=convb_sb[:, dtile:dtile + 1])
                        p.act(zch[:, dtile, :], pb[1][:, 0:TCH], AF.Silu)
                        p.mm(pb[2][0:65, 0:TCH], wxT_sb[:, dtile, :], uch[:, dtile, :],
                             start=(dtile == 0), stop=(dtile == NT_DI - 1))
                    if "a" in dbg and tc == 0:
                        p.dma(dbgo["d_u"], uch[:])
                        p.dma(dbgo["d_z"], zch[:])
                    p.copy(proj_sb[0:65, :], pb[2][0:65, 0:TCH])
                    p.dma(hb_dtr.unsqueeze(0), proj_sb[64:65, :])
                    p.dma(dtraw_bc[:], hb_dtr.unsqueeze(0).broadcast_to([128, TCH]))
                    p.copy(Bbf[0:32], proj_sb[0:32, :])
                    p.dma(hb_b.rearrange("(n t) -> n t", n=NST), Bbf[0:32])
                    p.dma(Brep[:], hb_b.rearrange("(n t) -> n t", n=NST)
                          .unsqueeze(0).broadcast_to([128, NST, TCH]))
                    p.copy(Bbf[32:64], proj_sb[32:64, :])
                    p.dma(hb_c.rearrange("(n t) -> n t", n=NST), Bbf[32:64])
                    p.dma(Crep[:], hb_c.rearrange("(n t) -> n t", n=NST)
                          .unsqueeze(0).broadcast_to([128, NST, TCH]))
                    if "b" in dbg and tc == 0:
                        p.dma(dbgo["d_proj"], proj_sb[0:65, :])
                        p.dma(dbgo["d_dtr"], dtraw_bc[:])
                        p.dma(dbgo["d_brep"], Brep[:])
                        p.dma(dbgo["d_crep"], Crep[:])
                    if tc > 0:
                        phase_d(tc - 1)
                    def dt_dta(dtile):
                        dt_t = dt_tR[dtile % 2]
                        dt_s = dt_sR[dtile % 2]
                        dta = dtaR[dtile % 2]
                        p.ts(dt_s[:], dtraw_bc[:], wdt_sb[:, dtile:dtile + 1],
                             bdt_sb[:, dtile:dtile + 1], AL.mult, AL.add)
                        p.ts(dt_s[:], dt_s[:], 80.0, None, AL.min)
                        p.act(dt_t[:], dt_s[:], AF.Exp)
                        p.act(dt_t[:], dt_t[:], AF.Ln, bias=1.0)
                        p.tt(dta[:, :, 1:TCH + 1],
                             dt_t[:].unsqueeze(1).broadcast_to([128, NST, TCH]),
                             AR_sb[:, dtile, :].unsqueeze(2).broadcast_to([128, NST, TCH]),
                             AL.mult)

                    dt_dta(0)
                    for dtile in range(NT_DI):
                        dta = dtaR[dtile % 2]
                        Xb = XbR[dtile % 2]
                        u1b = Xb[:, :, 1:TCH + 1]
                        hbuf = hbufR[dtile % 2]
                        dt_t = dt_tR[dtile % 2]
                        du_t = du_tR[dtile % 2]
                        y_t = y_tR[dtile % 2]
                        p.act(u1b, dta[:, :, 1:TCH + 1], AF.Ln, scale=-0.5, bias=C1P)
                        p.act(dta[:, :, 1:TCH + 1], dta[:, :, 1:TCH + 1], AF.Exp,
                              bias=lna_bc[:, 0:1])
                        p.act(u1b, u1b, AF.Exp, scale=-1.0, bias=l1ma_bc[:, 0:1])
                        if dtile + 1 < NT_DI:
                            dt_dta(dtile + 1)
                        p.tt(dta[:, :, 1:TCH + 1], dta[:, :, 1:TCH + 1], u1b, AL.add)
                        p.tt(du_t[:], dt_t[:], uch[:, dtile, :], AL.mult)
                        p.tt(Xb[:, :, 1:TCH + 1],
                             du_t[:].unsqueeze(1).broadcast_to([128, NST, TCH]),
                             Brep[:], AL.mult)
                        if tc == 0:
                            p.memset(Xb[:, :, 0:1], 0.0)
                        else:
                            p.copy(Xb[:, :, 0:1], hcarry[:, dtile, :].unsqueeze(2))
                        p.scan(hbuf[:].rearrange("p a b -> p (a b)"),
                               dta[:].rearrange("p a b -> p (a b)"),
                               Xb[:].rearrange("p a b -> p (a b)"),
                               0.0, AL.mult, AL.add)
                        if "c" in dbg and tc == 0 and dtile == 0:
                            p.dma(dbgo["d_dta"], dta[:])
                            p.dma(dbgo["d_x"], Xb[:])
                            p.dma(dbgo["d_h"], hbuf[:])
                            p.dma(dbgo["d_dt"], dt_t[:])
                        p.copy(hcarry[:, dtile, :], hbuf[:, :, TCH])
                        hc = dta[:, :, 1:TCH + 1]
                        p.tt(hc, hbuf[:, :, 1:TCH + 1], Crep[:], AL.mult)
                        _finish_unit(dta, y_t, dtile)
                    if "yg" in dbg and tc == 0:
                        p.dma(dbgo["d_yg"], ygch[:])
                phase_d(NCH - 1)
            p.fence()

        def quant_thunks(src, dst, W, qtmp):
            nb = W // 16
            mx, scl, inv, tq, ta, q, m = (qtmp[k] for k in
                                          ("mx", "scl", "inv", "tq", "ta", "q", "m"))
            th = []
            th.append(lambda: p.red(mx[:, 0:nb].unsqueeze(2),
                                    src.rearrange("p (a b) -> p a b", b=16),
                                    AX.X, AL.max, absval=True))
            th.append(lambda: p.ts(scl[:, 0:nb], mx[:, 0:nb], 1.0 / 6.0, 1e-12,
                                   AL.mult, AL.max))
            th.append(lambda: p.recip(inv[:, 0:nb], scl[:, 0:nb]))
            th.append(lambda: p.tt(tq[:, 0:W].rearrange("p (a b) -> p a b", b=16),
                                   src.rearrange("p (a b) -> p a b", b=16),
                                   inv[:, 0:nb].unsqueeze(2).broadcast_to([128, nb, 16]),
                                   AL.mult))
            th.append(lambda: p.act(ta[:, 0:W], tq[:, 0:W], AF.Abs))
            th.append(lambda: p.ts(q[:, 0:W], ta[:, 0:W], 5.0, 2.0,
                                   AL.is_gt, AL.mult))
            for c in (3.5, 2.5):
                th.append(lambda c=c: p.stt(q[:, 0:W], ta[:, 0:W], c, q[:, 0:W],
                                            AL.is_gt, AL.add))
            th.append(lambda: p.ts(q[:, 0:W], q[:, 0:W], 2.0, 0.0, AL.mult))
            for c in (1.75, 1.25, 0.75, 0.25):
                th.append(lambda c=c: p.stt(q[:, 0:W], ta[:, 0:W], c, q[:, 0:W],
                                            AL.is_gt, AL.add))
            th.append(lambda: p.act(ta[:, 0:W], tq[:, 0:W], AF.Sign))
            th.append(lambda: p.stt(tq[:, 0:W], q[:, 0:W], 0.5, ta[:, 0:W],
                                    AL.mult, AL.mult))
            th.append(lambda: p.tt(dst.rearrange("p (a b) -> p a b", b=16),
                                   tq[:, 0:W].rearrange("p (a b) -> p a b", b=16),
                                   scl[:, 0:nb].unsqueeze(2)
                                   .broadcast_to([128, nb, 16]), AL.mult))
            return th

        def quant(ph_ar, src, dst, W, qtmp):
            for t_ in quant_thunks(src, dst, W, qtmp):
                t_()

        def ffn0():
            with contextlib.ExitStack() as ph:
                def ar(name, shape, dt=F32):
                    return ph.enter_context(nc.sbuf_tensor(_mk(name), list(shape), dt))
                qtmp = {
                    "mx": ar("f_mx", [128, FI // 16]),
                    "scl": ar("f_scl", [128, FI // 16]),
                    "inv": ar("f_inv", [128, FI // 16]),
                    "tq": ar("f_tq", [128, FI]),
                    "ta": ar("f_ta", [128, FI]),
                    "q": ar("f_q", [128, FI], BF16),
                    "m": ar("f_m", [128, FI], BF16),
                }
                xqTR = [ar("f_xqT", [128, KT_D, 128], BF16) for _ in range(2)]
                wdqT = ar("f_wdqT", [128, NIT, 1024], BF16)
                xtm = ar("f_xtm", [128, D])
                xqtm = ar("f_xqtm", [128, D], BF16)
                wrow = ar("f_wrow", [128, FI])
                wql = ar("f_wql", [128, FI], BF16)
                wgqTcR = [ar("f_wgq", [128, KT_D, 512], BF16) for _ in range(2)]
                wuqTcR = [ar("f_wuq", [128, KT_D, 512], BF16) for _ in range(2)]
                tmpfR = [ar("f_tmpf", [128, 512]) for _ in range(2)]

                hchkR = [ar("f_hchk", [128, 512]) for _ in range(2)]
                htm = ar("f_htm", [128, FI])
                hqtm = ar("f_hqtm", [128, FI], BF16)
                hqTR = None
                ffntm = None
                ffndm = wrow[:, 0:D].rearrange("p (a c) -> p a c", a=KT_D)
                hqTR = [ar("f_hqT", [128, NIT, 128], BF16),
                        wgqTcR[0][:].rearrange("p a c -> p (a c)")[:, 0:NIT * 128]
                        .rearrange("p (a c) -> p a c", c=128)]
                x1c = wrow[:, D:2 * D].rearrange("p (a c) -> p a c", a=KT_D)
                ffntm = htm[:, 0:D]

                for t in range(16):
                    ts0 = t * 128
                    xqT = xqTR[t % 2]
                    p.dma(x1c[:], fw.rearrange("(a p) t -> p a t", p=128)[:, :, ts0:ts0 + 128],
                          tag="xnf32")
                    for k in range(KT_D):
                        tposef(pb[5][:, 0:128], x1c[:, k, :])
                        p.copy(xtm[:, k * 128:(k + 1) * 128], pb[5][:, 0:128], eng="scalar")
                    quant(ph, xtm[:], xqtm[:], D, qtmp)
                    for k in range(KT_D):
                        tposeb(pbb(5), xqtm[:, k * 128:(k + 1) * 128])
                        p.copy(xqT[:, k, :], pbb(5))
                    p.dma(xq_hbm[t, :, :, :], xqT[:], tag="xqst")
                for rt in range(KT_D):
                    p.dma(wrow[:], wd_in[rt * 128:(rt + 1) * 128, :], tag="wdld")
                    quant(ph, wrow[:], wql[:], FI, qtmp)
                    for It in range(NIT):
                        tposeb(pbb(5), wql[:, It * 128:(It + 1) * 128])
                        p.copy(wdqT[:, It, rt * 128:(rt + 1) * 128], pbb(5),
                               eng="scalar")
                ic_sizes = [512] * 5 + [256]

                def g_wprep_thunks(ic):
                    Iw = ic_sizes[ic]
                    i0 = ic * 512
                    wgqTc = wgqTcR[ic % 2]
                    wuqTc = wuqTcR[ic % 2]
                    th = []
                    for rt in range(Iw // 128):
                        r0 = i0 + rt * 128
                        th.append(lambda r0=r0: p.dma(wrow[:, 0:D],
                                                      wg_in[r0:r0 + 128, :], tag="wgld"))
                        th += quant_thunks(wrow[:, 0:D], wql[:, 0:D], D, qtmp)
                        for k in range(KT_D):
                            th.append(lambda k=k: tposeb(pbb(5),
                                                         wql[:, k * 128:(k + 1) * 128]))
                            th.append(lambda k=k, rt=rt, wgqTc=wgqTc:
                                      p.copy(wgqTc[:, k, rt * 128:(rt + 1) * 128],
                                             pbb(5)))
                        th.append(lambda r0=r0: p.dma(wrow[:, 0:D],
                                                      wu_in[r0:r0 + 128, :], tag="wuld"))
                        th += quant_thunks(wrow[:, 0:D], wql[:, 0:D], D, qtmp)
                        for k in range(KT_D):
                            th.append(lambda k=k: tposeb(pbb(5),
                                                         wql[:, k * 128:(k + 1) * 128]))
                            th.append(lambda k=k, rt=rt, wuqTc=wuqTc:
                                      p.copy(wuqTc[:, k, rt * 128:(rt + 1) * 128],
                                             pbb(5), eng="scalar"))
                    return th

                def g_mm_thunks(ic):
                    Iw = ic_sizes[ic]
                    i0 = ic * 512
                    wgqTc = wgqTcR[ic % 2]
                    wuqTc = wuqTcR[ic % 2]
                    th = []
                    for t in range(16):
                        xqT = xqTR[t % 2]
                        tmpf = tmpfR[t % 2]
                        hchk = hchkR[t % 2]
                        th.append(lambda t=t, xqT=xqT:
                                  p.dma(xqT[:], xq_hbm[t, :, :, :], tag="xqld"))
                        bg, bu = [(pb[0], pb[1]), (pb[2], pb[3]),
                                  (pb[6], pb[7])][t % 3]
                        for k in range(KT_D):
                            th.append(lambda k=k, bg=bg, xqT=xqT, wgqTc=wgqTc, Iw=Iw:
                                      p.mm(bg[:, 0:Iw], xqT[:, k, :], wgqTc[:, k, 0:Iw],
                                           start=(k == 0), stop=(k == KT_D - 1)))
                            th.append(lambda k=k, bu=bu, xqT=xqT, wuqTc=wuqTc, Iw=Iw:
                                      p.mm(bu[:, 0:Iw], xqT[:, k, :], wuqTc[:, k, 0:Iw],
                                           start=(k == 0), stop=(k == KT_D - 1)))
                        th.append(lambda bg=bg, tmpf=tmpf, Iw=Iw:
                                  p.act(tmpf[:, 0:Iw], bg[:, 0:Iw], AF.Silu))
                        th.append(lambda bu=bu, tmpf=tmpf, hchk=hchk, Iw=Iw:
                                  p.tt(hchk[:, 0:Iw], tmpf[:, 0:Iw], bu[:, 0:Iw],
                                       AL.mult))
                        th.append(lambda t=t, hchk=hchk, Iw=Iw, i0=i0:
                                  p.dma(h_hbm[t * 128:(t + 1) * 128, i0:i0 + Iw],
                                        hchk[:, 0:Iw], tag="hst"))
                    return th

                import itertools as _it
                for t_ in g_wprep_thunks(0):
                    t_()
                for ic in range(6):
                    mms = g_mm_thunks(ic)
                    wps = g_wprep_thunks(ic + 1) if ic + 1 < 6 else []
                    for a, b in _it.zip_longest(mms, wps):
                        if a:
                            a()
                        if b:
                            b()
                def h_prep_thunks(t):
                    hqT = hqTR[t % 2]
                    th = [lambda t=t: p.dma(htm[:], h_hbm[t * 128:(t + 1) * 128, :],
                                            tag="hld")]
                    th += quant_thunks(htm[:], hqtm[:], FI, qtmp)
                    for It in range(NIT):
                        th.append(lambda It=It: tposeb(pbb(5),
                                                       hqtm[:, It * 128:(It + 1) * 128]))
                        th.append(lambda It=It, hqT=hqT: p.copy(hqT[:, It, :], pbb(5)))
                    return th

                def h_mm_thunks(t):
                    ts0 = t * 128
                    hqT = hqTR[t % 2]
                    bg, bu = (pb[0], pb[1]) if t % 2 == 0 else (pb[2], pb[3])
                    th = []
                    for It in range(NIT):
                        th.append(lambda It=It, bg=bg, hqT=hqT:
                                  p.mm(bg[:, 0:512], hqT[:, It, :], wdqT[:, It, 0:512],
                                       start=(It == 0), stop=(It == NIT - 1)))
                        th.append(lambda It=It, bu=bu, hqT=hqT:
                                  p.mm(bu[:, 0:512], hqT[:, It, :], wdqT[:, It, 512:1024],
                                       start=(It == 0), stop=(It == NIT - 1)))
                    th.append(lambda bg=bg: p.copy(ffntm[:, 0:512], bg[:, 0:512]))
                    th.append(lambda bu=bu: p.copy(ffntm[:, 512:1024], bu[:, 0:512],
                                                   eng="scalar"))
                    for k in range(KT_D):
                        th.append(lambda k=k: tposef(pb[4][:, 0:128],
                                                     ffntm[:, k * 128:(k + 1) * 128]))
                        th.append(lambda k=k: p.copy(ffndm[:, k, :], pb[4][:, 0:128]))
                    th.append(lambda ts0=ts0:
                              p.dma(x1c[:], x1.rearrange("(a p) t -> p a t", p=128)
                                    [:, :, ts0:ts0 + 128], tag="x1ld"))
                    th.append(lambda: p.tt(x1c[:], x1c[:], ffndm[:], AL.add))
                    th.append(lambda ts0=ts0:
                              p.dma(x1.rearrange("(a p) t -> p a t", p=128)
                                    [:, :, ts0:ts0 + 128], x1c[:], tag="x1st2"))
                    return th

                for t_ in h_prep_thunks(0):
                    t_()
                for t in range(16):
                    mms = h_mm_thunks(t)
                    wps = h_prep_thunks(t + 1) if t + 1 < 16 else []
                    for a, b in _it.zip_longest(mms, wps):
                        if a:
                            a()
                        if b:
                            b()
            p.fence()

        def gdn(with_ffn1=True):
            with contextlib.ExitStack() as ph:
                def ar(name, shape, dt=F32):
                    return ph.enter_context(nc.sbuf_tensor(_mk(name), list(shape), dt))
                qk_ch = ar("g_qkch", [128, T], BF16)
                v_ch = ar("g_vch", [128, T], BF16)
                Kc_sbs = [None, None]
                Qc_sbs = [None, None]
                wstgb = ar("g_wstgb", [128, 8, 128], BF16)
                qlin = ar("g_qlin", [128, T + 3])
                cacc = ar("g_cacc", [128, T])
                sq_t0 = ar("g_sq0", [128, T])
                cwq_sb = ar("g_cwq", [128, 8, 4])
                cwk_sb = ar("g_cwk", [128, 8, 4])
                cwv_sb = ar("g_cwv", [128, 16, 4])
                sq_t = sq_t0[:, 0:T]
                wstg = sq_t0[:, 0:1024].rearrange("p (a c) -> p a c", a=8)
                nrm = ar("g_nrm", [2, T])
                nvb = qlin[:, 0:T]
                rowbuf = ar("g_rowbuf", [128, 2 * T])
                srow = rowbuf[0:16, 0:T]
                brow = rowbuf[0:16, T:2 * T]
                sgt = ar("g_sgt", [128, 2 * T])
                sig = sgt[0:16, 0:T]
                tau = sgt[0:16, T:2 * T]
                crst = ar("g_crst", [16, T], BF16)
                sp1 = cacc[0:16, 0:T]
                wab = ar("g_wab", [128, 8, 32], BF16)
                ealx = ar("g_eal", [16, 1])
                dtb_sb = ar("g_dtb", [16, 1])
                sigP = cacc[:, 0:T]
                sigT = ar("g_sigT", [128, GC, 16])
                nsigT = ar("g_nsigT", [128, GC, 16])
                tauT = ar("g_tauT", [128, GC, 16])
                seB = ar("g_seB", [128, GH, GC])
                gLcol = ar("g_gL", [128, GH, GC])
                sigBs = [qlin[:, 0:T], cacc[:, 0:T], sq_t[:, 0:T]]
                tauBs = [rowbuf[:, 0:T], rowbuf[:, T:2 * T],
                         ar("g_tau3", [128, T])[:]]
                glsAs = [ar("g_glsA", [128, GC]) for _ in range(3)]
                gam64s = [ar("g_gam64", [128, 128]) for _ in range(3)]
                mbls = ar("g_mbls", [128, 128], BF16)
                mbus = ar("g_mbus", [128, 128], BF16)
                mbui = ar("g_mbui", [128, 128], BF16)

                def prs(name, shape, dt=F32):
                    return [ar(name, shape, dt) for _ in range(3)]
                eargs = prs("g_earg", [128, 128])
                Emats = prs("g_E", [128, 128])
                EmatTs = prs("g_ET", [128, 128])

                P_sbs = prs("g_P", [128, 128], BF16)
                PT_sbs = prs("g_PT", [128, 128], BF16)
                P2_sbs = prs("g_P2", [128, 128], BF16)
                P2T_sbs = prs("g_P2T", [128, 128], BF16)
                W_sbs = prs("g_W", [128, 128], BF16)
                Vt_sbs = prs("g_Vt", [128, 128], BF16)
                Vc_sbs = prs("g_vc2", [128, 128], BF16)
                Kc_sbs = prs("g_kc", [128, 128], BF16)
                Qc_sbs = prs("g_qc", [128, 128], BF16)
                At_sbs = prs("g_At", [128, 128], BF16)
                Qg_sbs = prs("g_Qg", [128, 128], BF16)
                KpT_sbs = prs("g_KpT", [128, 64], BF16)
                dgl_sbs = prs("g_dgl", [128, 64], BF16)
                S_bfs = prs("g_Sbf", [128, 8, 128], BF16)
                o_sbs = prs("g_osb", [128, 128])
                rsins = prs("g_rsin", [128, 1])

                o_sts = prs("g_ost", [128, 128], BF16)

                e2 = ar("g_e2", [128, 2])

                p.dma(eargs[0][:], mb_ls)
                p.copy(mbls[:], eargs[0][:])
                p.dma(eargs[0][:], mb_us)
                p.copy(mbus[:], eargs[0][:])
                p.dma(eargs[0][:], mb_ui)
                p.copy(mbui[:], eargs[0][:])
                p.dma(crst[:], creset, eng="gpsimd")
                p.dma(cwq_sb[:], cq_in)
                p.dma(cwk_sb[:], ck_in)
                p.dma(cwv_sb[:], cv_in)

                def proj_conv(wT, ct, cw_sb, cwi, dst, dst_i):
                    p.dma(wstg, wT.rearrange("(a p) c -> p a c", p=128)
                          [:, :, ct * 128:(ct + 1) * 128], tag="gw")
                    p.copy(wstgb[:], wstg)
                    for nn in range(4):
                        bank = pb[nn % 2]
                        for kt in range(KT_D):
                            p.mm(bank[:, 0:512], wstgb[:, kt, :],
                                 xn[:, kt, nn * 512:(nn + 1) * 512],
                                 start=(kt == 0), stop=(kt == KT_D - 1))
                        p.copy(qlin[:, 3 + nn * 512:3 + (nn + 1) * 512], bank[:, 0:512])
                    p.memset(qlin[:, 0:3], 0.0)
                    p.ts(cacc[:], qlin[:, 0:T], cw_sb[:, cwi, 0:1], None, AL.mult)
                    for j in range(1, 4):
                        p.stt(cacc[:], qlin[:, j:j + T], cw_sb[:, cwi, j:j + 1],
                              cacc[:], AL.mult, AL.add)
                    p.act(dst[:, dst_i, :], cacc[:], AF.Silu)

                def qk_tile(wT, cw_sb, ct, dst_hbm):
                    proj_conv(wT, ct, cw_sb, ct, qk_ch[:].unsqueeze(1), 0)
                    p.act(sq_t[:], qk_ch[:], AF.Square)
                    for nn in range(4):
                        p.mm(pb[2][0:2, 0:512], e2[:], sq_t[:, nn * 512:(nn + 1) * 512],
                             start=True, stop=True)
                        p.copy(nrm[0:2, nn * 512:(nn + 1) * 512], pb[2][0:2, 0:512])
                    p.ts(nrm[:], nrm[:], 1e-12, None, AL.max)
                    p.act(nrm[:], nrm[:], AF.Ln)
                    p.act(nrm[:], nrm[:], AF.Exp, scale=-0.5)
                    p.dma(hb_nv.rearrange("(a t) -> a t", a=2), nrm[0:2, :])
                    p.dma(nvb, hb_nv.rearrange("(a t) -> a t", a=2)
                          .unsqueeze(1).broadcast_to([2, 64, T]))
                    p.tt(qk_ch[:], qk_ch[:], nvb, AL.mult)
                    p.dma(dst_hbm[ct, :, :], qk_ch[:], tag="qkst")

                p.memset(e2[:], 0.0)
                p.memset(e2[0:64, 0:1], 1.0)
                p.memset(e2[64:128, 1:2], 1.0)
                for ct in range(8):
                    qk_tile(wqT, cwq_sb, ct, q_hbm)
                for ct in range(8):
                    qk_tile(wkT, cwk_sb, ct, k_hbm)
                for ct in range(16):
                    proj_conv(wvT, ct, cwv_sb, ct, v_ch[:].unsqueeze(1), 0)
                    p.dma(v_hbm[ct, :, :], v_ch[:], tag="vst")


                if "g" in dbg:
                    p.dma(dbgo["g_qd"], q_dm[:])
                    p.dma(dbgo["g_kd"], k_dm[:])
                p.dma(wstg[:, :, 0:16], waT.rearrange("(a p) c -> p a c", p=128), tag="wa2")
                p.copy(wab[:, :, 0:16], wstg[:, :, 0:16])
                p.dma(wstg[:, :, 0:16], wbT.rearrange("(a p) c -> p a c", p=128), tag="wb2")
                p.copy(wab[:, :, 16:32], wstg[:, :, 0:16])
                for nn in range(4):
                    for kt in range(KT_D):
                        p.mm(pb[0][0:16, 0:512], wab[:, kt, 0:16],
                             xn[:, kt, nn * 512:(nn + 1) * 512],
                             start=(kt == 0), stop=(kt == KT_D - 1))
                        p.mm(pb[1][0:16, 0:512], wab[:, kt, 16:32],
                             xn[:, kt, nn * 512:(nn + 1) * 512],
                             start=(kt == 0), stop=(kt == KT_D - 1))
                    p.copy(srow[:, nn * 512:(nn + 1) * 512], pb[0][0:16, 0:512])
                    p.copy(brow[:, nn * 512:(nn + 1) * 512], pb[1][0:16, 0:512], eng="scalar")
                p.dma(dtb_sb[:], dtb1.unsqueeze(1), slow_ok=True)
                p.dma(ealx[:], al1.unsqueeze(1), slow_ok=True)
                p.act(ealx[:], ealx[:], AF.Exp)
                p.ts(ealx[:], ealx[:], -1.0, None, AL.mult)
                p.ts(srow[:], srow[:], dtb_sb[:, 0:1], 80.0, AL.add, AL.min)
                p.act(sp1, srow[:], AF.Exp)
                p.act(sp1, sp1, AF.Ln, bias=1.0)
                p.ts(srow[:], sp1, ealx[:, 0:1], None, AL.mult)
                p.act(brow[:], brow[:], AF.Sigmoid)
                p.act(brow[:], brow[:], AF.Ln)
                p.ts(brow[:], brow[:], 1.0, math.log(2.0), AL.mult, AL.add)
                p.scan(sig, crst[:], srow[:], 0.0, AL.mult, AL.add)
                p.tt(tau, sig, brow[:], AL.add)
                p.memset(sigP, 0.0)
                p.copy(sigP[0:16, :], sig)
                for c in range(GC):
                    tposef(pb[2][:, 0:128], sigP[:, c * 128:(c + 1) * 128])
                    p.copy(sigT[:, c, :], pb[2][:, 0:16])
                    p.ts(nsigT[:, c, :], pb[2][:, 0:16], -1.0, None, AL.mult)
                p.copy(sigP[0:16, :], tau)
                for c in range(GC):
                    tposef(pb[2][:, 0:128], sigP[:, c * 128:(c + 1) * 128])
                    p.copy(tauT[:, c, :], pb[2][:, 0:16])
                if "g" in dbg:
                    p.dma(dbgo["g_sg"], sig)
                    p.dma(dbgo["g_ta"], tau)
                p.dma(sg_hbm[0, :, :], sig)
                p.dma(sg_hbm[1, :, :], tau)
                p.copy(sigP[0:16, 0:GC], sig[:, GL - 1::GL])
                p.dma(hb_se.rearrange("(h c) -> h c", h=16), sigP[0:16, 0:GC])
                p.dma(seB[:], hb_se.rearrange("(h c) -> h c", h=16)
                      .unsqueeze(0).broadcast_to([128, GH, GC]))
                p.act(gLcol[:], seB[:], AF.Exp)
                p.memset(S_bfs[0][:], 0.0)
                p.memset(S_bfs[1][:], 0.0)
                p.memset(S_bfs[2][:], 0.0)

                hbrows_s = [(hb_row1, hb_row2), (hb_row3, hb_row4),
                            (hb_row5, hb_row6)]

                def head_prep(h):
                    st_ = h % 3
                    sigB, tauB = sigBs[st_], tauBs[st_]
                    r1, r2 = hbrows_s[st_]
                    p.dma(r1.unsqueeze(0), sg_hbm[0, h:h + 1, :])
                    p.dma(sigB, r1.unsqueeze(0).broadcast_to([128, T]))
                    p.dma(r2.unsqueeze(0), sg_hbm[1, h:h + 1, :])
                    p.dma(tauB, r2.unsqueeze(0).broadcast_to([128, T]))
                    p.tt(glsAs[st_][:], nsigT[:, :, h], seB[:, h, :], AL.add)
                    p.act(glsAs[st_][:], glsAs[st_][:], AF.Exp)

                def head_chunk(h, c):
                    ops = []
                    em = ops.append
                    st_ = h % 3
                    ct, hrow = h // 2, (h % 2) * 64
                    bG, bR = [(pb[0], pb[1]), (pb[4], pb[5]), (pb[2], pb[6])][st_]
                    bA = bG
                    bB = bR
                    bGb = bG[:, 0:64].bitcast(BF16)
                    bRb = bR[:, 0:64].bitcast(BF16)
                    sigB, tauB = sigBs[st_], tauBs[st_]
                    earg, Emat, EmatT = eargs[st_], Emats[st_], EmatTs[st_]
                    EmatA = Emat
                    P_sb, PT_sb, P2_sb, P2T_sb = (P_sbs[st_], PT_sbs[st_],
                                                  P2_sbs[st_], P2T_sbs[st_])
                    W_sb, Vt_sb, Vc_sb, At_sb = (W_sbs[st_], Vt_sbs[st_],
                                                 Vc_sbs[st_], At_sbs[st_])
                    KpT_sb, o_sb, rsin = KpT_sbs[st_], o_sbs[st_], rsins[st_]
                    o_n, o_st, gam64 = P2T_sbs[st_], o_sts[st_], gam64s[st_]
                    osq = eargs[st_]
                    glsA = glsAs[st_]
                    Qg_sb, dgl_sb, S_bf = Qg_sbs[st_], dgl_sbs[st_], S_bfs[st_]
                    csl = slice(c * 128, (c + 1) * 128)
                    Kc_b, Qc_b = Kc_sbs[st_], Qc_sbs[st_]
                    Kc = Kc_b[hrow:hrow + 64, :]
                    Qc = Qc_b[hrow:hrow + 64, :]
                    em(lambda: p.dma(Kc, k_hbm[ct, hrow:hrow + 64, csl], tag="kld"))
                    em(lambda: p.dma(Qc, q_hbm[ct, hrow:hrow + 64, csl], tag="qld"))
                    em(lambda: p.dma(Vc_sb[:], v_hbm[h, :, csl], tag="vld"))
                    S0b = S_bf[hrow:hrow + 64, h // 3, :]
                    em(lambda: p.mm(bG[:, 0:128], Kc, Kc, start=True, stop=True))
                    em(lambda: p.tt(earg[:], mbls[:], sigB[:, csl], AL.subtract))
                    em(lambda: p.act(Emat[:], earg[:], AF.Exp, bias=tauT[:, c, h:h + 1]))
                    em(lambda: p.tt(earg[:], mbus[:], tauB[:, csl], AL.add))
                    em(lambda: p.act(EmatT[:], earg[:], AF.Exp, bias=nsigT[:, c, h:h + 1]))
                    em(lambda: p.stt(P_sb[:], bG[:, 0:128], -1.0, Emat[:], AL.mult, AL.mult))
                    em(lambda: p.stt(PT_sb[:], bG[:, 0:128], -1.0, EmatT[:], AL.mult, AL.mult))
                    em(lambda: p.tt(earg[:], mbui[:], sigB[:, csl], AL.add))
                    em(lambda: p.act(EmatA[:], earg[:], AF.Exp, bias=nsigT[:, c, h:h + 1]))
                    em(lambda: p.transpose(bRb[:, 0:128], Vc_sb[:], iden_bf[:]))
                    em(lambda: p.copy(Vt_sb[:], bRb[:, 0:128]))
                    em(lambda: p.mm(bR[:, 0:128], Kc, S0b, start=True, stop=True))
                    em(lambda: p.act(rsin[:], tauT[:, c, h:h + 1], AF.Exp))
                    em(lambda: p.ts(rsin[:], rsin[:], -1.0, None, AL.mult))
                    em(lambda: p.stt(W_sb[:], bR[:, 0:128], rsin[:, 0:1], Vt_sb[:],
                                     AL.mult, AL.add))
                    pp = [(P_sb, PT_sb), (P2_sb, P2T_sb)]
                    for j in range(7):
                        Pa, PTa = pp[j % 2]
                        Pb, PTb = pp[1 - j % 2]
                        em(lambda PTa=PTa: p.mm(bA[:, 0:128], PTa[:], W_sb[:],
                                                start=True, stop=True))
                        em(lambda: p.tt(W_sb[:], W_sb[:], bA[:, 0:128], AL.add))
                        if j < 6:
                            em(lambda PTa=PTa, Pa=Pa: p.mm(bB[:, 0:128], PTa[:], Pa[:],
                                                           start=True, stop=True))
                            em(lambda Pb=Pb: p.copy(Pb[:], bB[:, 0:128]))
                            em(lambda Pa=Pa, PTa=PTa: p.mm(bB[:, 0:128], Pa[:], PTa[:],
                                                           start=True, stop=True))
                            em(lambda PTb=PTb: p.copy(PTb[:], bB[:, 0:128], eng="scalar"))
                    em(lambda: p.mm(bG[:, 0:128], Kc, Qc, start=True, stop=True))
                    em(lambda: p.stt(At_sb[:], bG[:, 0:128], 1.0, EmatA[:], AL.mult, AL.mult))
                    em(lambda: p.mm(bG[:, 0:128], At_sb[:], W_sb[:], start=True, stop=False))
                    em(lambda: p.act(gam64[hrow:hrow + 64, :], sigB[hrow:hrow + 64, csl],
                                     AF.Exp))
                    em(lambda: p.tt(Qg_sb[hrow:hrow + 64, :], Qc,
                                    gam64[hrow:hrow + 64, :], AL.mult))
                    em(lambda: p.mm(bG[:, 0:128], Qg_sb[hrow:hrow + 64, :], S0b,
                                    start=False, stop=True))
                    em(lambda: p.copy(o_sb[:], bG[:, 0:128]))
                    em(lambda: p.act(osq[:], o_sb[:], AF.Square, accum_out=rsin[:, 0:1]))
                    em(lambda: p.act(rsin[:], rsin[:], AF.Ln, scale=1.0 / 128.0, bias=1e-6))
                    em(lambda: p.act(rsin[:], rsin[:], AF.Exp, scale=-0.5))
                    em(lambda: p.ts(o_n[:], o_sb[:], rsin[:, 0:1], None, AL.mult))
                    em(lambda: p.transpose(bRb[:, 0:128], o_n[:], iden_bf[:]))
                    em(lambda: p.copy(o_st[:], bRb[:, 0:128]))
                    em(lambda: p.dma(o_hbm[h * 128:(h + 1) * 128, csl], o_st[:], tag="ost"))
                    em(lambda: p.transpose(bRb[:, 0:64], Kc,
                                           iden_bf[hrow:hrow + 64, hrow:hrow + 64]))
                    em(lambda: p.ts(KpT_sb[:, 0:64], bRb[:, 0:64], glsA[:, c:c + 1],
                                    None, AL.mult))
                    em(lambda: p.ts(dgl_sb[hrow:hrow + 64, :],
                                    iden_bf[hrow:hrow + 64, hrow:hrow + 64],
                                    gLcol[hrow:hrow + 64, h, c:c + 1], None, AL.mult))
                    em(lambda: p.mm(bR[hrow:hrow + 64, 0:128], KpT_sb[:, 0:64], W_sb[:],
                                    start=True, stop=False, tile_position=(0, hrow)))
                    em(lambda: p.mm(bR[hrow:hrow + 64, 0:128], dgl_sb[hrow:hrow + 64, :],
                                    S0b, start=False, stop=True,
                                    tile_position=(hrow, hrow)))
                    em(lambda: p.copy(S0b, bR[hrow:hrow + 64, 0:128]))
                    if "g" in dbg and h == 0 and c == 0:
                        em(lambda: p.dma(dbgo["g_P0"], P_sb[:]))
                        em(lambda: p.dma(dbgo["g_Wd"], W_sb[:]))
                        em(lambda: p.dma(dbgo["g_obd"], o_sb[:]))
                        em(lambda: p.dma(dbgo["g_Sd"], S_bf[:]))
                        em(lambda: p.dma(dbgo["g_ost"], o_st[:]))
                        em(lambda: p.dma(dbgo["g_P0T"], PT_sb[:]))
                        em(lambda: p.dma(dbgo["g_vt"], Vt_sb[:]))
                    return ops

                ffn_slices = (ffn1_make(ar, sgt, qk_ch, v_ch)
                              if with_ffn1 else [])
                import itertools
                nsl_total = len(ffn_slices)
                nrounds = (GH + 2) // 3
                for rnd in range(nrounds):
                    hs = [h for h in (3 * rnd, 3 * rnd + 1, 3 * rnd + 2) if h < GH]
                    for h in hs:
                        head_prep(h)
                    for c in range(GC):
                        tls = [head_chunk(h, c) for h in hs]
                        for grp in itertools.zip_longest(*tls):
                            for g in grp:
                                if g:
                                    g()
                    want = nsl_total * (rnd + 1) // nrounds
                    while ffn_slices and nsl_total - len(ffn_slices) < want:
                        ffn_slices.pop(0)()
                while ffn_slices:
                    ffn_slices.pop(0)()
            p.fence()

            with contextlib.ExitStack() as ph:
                def ar(name, shape, dt=F32):
                    return ph.enter_context(nc.sbuf_tensor(_mk(name), list(shape), dt))
                wstg2 = ar("g2_wstg", [128, 16, 1024])
                wstgb2 = ar("g2_wstgb", [128, 16, 1024], BF16)
                orh = ar("g2_orh", [128, 16, 512], BF16)
                outsb = ar("g2_outsb", [128, 512])
                fwsb = ar("g2_fwsb", [128, 512])
                p.dma(wstg2[:], woT.rearrange("(a p) c -> p a c", p=128), tag="wo")
                p.copy(wstgb2[:], wstg2[:])
                for nn in range(4):
                    nsl = slice(nn * 512, (nn + 1) * 512)
                    p.dma(orh[:], o_hbm.rearrange("(a p) t -> p a t", p=128)[:, :, nsl],
                          tag="old")
                    for i in range(KT_D):
                        bank = pb[i % 2]
                        for kt in range(16):
                            p.mm(bank[:, 0:512],
                                 wstgb2[:, kt, i * 128:(i + 1) * 128], orh[:, kt, :],
                                 start=(kt == 0), stop=(kt == 15))
                        p.dma(outsb[:], x1[i * 128:(i + 1) * 128, nsl], tag="x1ld2")
                        p.dma(fwsb[:], fw[i * 128:(i + 1) * 128, nsl], tag="fwld3")
                        p.stt(outsb[:], bank[:, 0:512], 1.0, outsb[:], AL.mult, AL.add)
                        p.tt(outsb[:], outsb[:], fwsb[:], AL.add)
                        p.dma(out_hbm[i * 128:(i + 1) * 128, nsl], outsb[:], tag="outst")
            p.fence()

        def ffn1_make(ar, sgt, qk_ch, v_ch):
            if True:
                gw1_sb = ar("l_gw1", [128, 8, 512], BF16)
                gw2_sb = ar("l_gw2", [128, 4, 256], BF16)
                gw3_sb = ar("l_gw3", [128, 2, 1], BF16)
                gb1_sb = ar("l_gb1", [128, 4])
                gb2_sb = ar("l_gb2", [128, 2])
                gb3_sb = ar("l_gb3", [1, 1])
                cw1_sb = ar("l_cw1", [128, 8, 1024], BF16)
                cw2_sb = ar("l_cw2", [128, 8, 512], BF16)
                cw3_sb = ar("l_cw3", [128, 4, 256], BF16)
                cw4_sb = ar("l_cw4", [128, 2, 1], BF16)
                cb1_sb = ar("l_cb1", [128, 8])
                cb2_sb = ar("l_cb2", [128, 4])
                cb3_sb = ar("l_cb3", [128, 2])
                cb4_sb = ar("l_cb4", [1, 1])
                ar_sb = ar("l_ar", [1, 4])
                gcv = ar("l_gcv", [128, 8])
                gcb = ar("l_gcb", [128, 8], BF16)
                gh1 = ar("l_gh1", [128, 4], BF16)
                gh2 = ar("l_gh2", [128, 2], BF16)
                gf_sb = ar("l_gf", [1, 2])
                fwc = sgt[:, 0:2 * T].rearrange("p (a c) -> p a c", a=8)[:, :, 0:512]
                fwb = ar("l_fwb", [128, 8, 512], BF16)
                h1b = ar("l_h1b", [128, 8, 512], BF16)
                h2b = qk_ch[:].rearrange("p (a c) -> p a c", a=4)
                h3b = v_ch[:, 0:1024].rearrange("p (a c) -> p a c", a=2)
                cpp = v_ch[:, 1024:2048].bitcast(F32)
                tfr = cpp[0:1, 0:512]
                tmpt = xn[:, 0, 0:1024].bitcast(F32)

                def ldw(dst_bf, src, kt_n, width):
                    p.dma(dst_bf[:, 0:kt_n, 0:width],
                          src.rearrange("(a p) c -> p a c", p=128), tag="lw",
                          slow_ok=(width < 16), eng="gpsimd")

                ldw(cw1_sb, cw1, 8, 1024)
                ldw(cw2_sb, cw2, 8, 512)
                ldw(cw3_sb, cw3, 4, 256)
                ldw(cw4_sb, cw4, 2, 1)
                ldw(gw1_sb, gw1, 8, 512)
                ldw(gw2_sb, gw2, 4, 256)
                ldw(gw3_sb, gw3, 2, 1)
                p.dma(cb1_sb[:], cb1)
                p.dma(cb2_sb[:, 0:4], cb2)
                p.dma(cb3_sb[:, 0:2], cb3)
                p.dma(cb4_sb[0:1, 0:1], cb4.unsqueeze(0))
                p.dma(gb1_sb[:], gb1)
                p.dma(gb2_sb[:, 0:2], gb2)
                p.dma(gb3_sb[0:1, 0:1], gb3.unsqueeze(0))
                p.dma(ar_sb[0:1, 0:1], ar_in.unsqueeze(0))
                p.ts(ar_sb[0:1, 1:2], ar_sb[0:1, 0:1], 2.0, None, AL.mult)
                p.ts(ar_sb[0:1, 2:3], ar_sb[0:1, 0:1], -1.0, 0.1, AL.mult, AL.add)

                for kt in range(KT_D):
                    p.red(gcv[:, kt:kt + 1], xn[:, kt, :], AX.X, AL.add)
                p.ts(gcv[:], gcv[:], 1.0 / T, None, AL.mult)
                p.copy(gcb[:], gcv[:])
                for m in range(4):
                    bk = pb[3] if m % 2 == 0 else pb[7]
                    for k in range(KT_D):
                        p.mm(bk[0:128, 0:1], gw1_sb[:, k, m * 128:(m + 1) * 128],
                             gcb[:, k:k + 1], start=(k == 0), stop=(k == KT_D - 1))
                    p.act(gh1[:, m:m + 1], bk[0:128, 0:1], AF.Gelu,
                          bias=gb1_sb[:, m:m + 1])
                for m in range(2):
                    bk = pb[3] if m % 2 == 0 else pb[7]
                    for k in range(4):
                        p.mm(bk[0:128, 0:1], gw2_sb[:, k, m * 128:(m + 1) * 128],
                             gh1[:, k:k + 1], start=(k == 0), stop=(k == 3))
                    p.act(gh2[:, m:m + 1], bk[0:128, 0:1], AF.Gelu,
                          bias=gb2_sb[:, m:m + 1])
                for k in range(2):
                    p.mm(pb[3][0:1, 0:1], gw3_sb[:, k, 0:1], gh2[:, k:k + 1],
                         start=(k == 0), stop=(k == 1))
                p.act(gf_sb[0:1, 0:1], pb[3][0:1, 0:1], AF.Sigmoid,
                      bias=gb3_sb[0:1, 0:1])
                p.ts(gf_sb[0:1, 1:2], gf_sb[0:1, 0:1], 0.7, None, AL.mult)

                tsrc = [4, 5, 6, 7, 0, 1, 2, 3]
                tsgn = [1.0, 1.0, -1.0, -1.0, -1.0, -1.0, 1.0, 1.0]
                slices = []

                def slice_body(cyc, nn):
                    if True:
                        nsl = slice(nn * 512, (nn + 1) * 512)
                        for kt in range(KT_D):
                            p.dma(fwc[:, kt, :], fw[kt * 128:(kt + 1) * 128, nsl],
                                  tag="fwld")
                        p.copy(fwb[:], fwc[:])
                        for m in range(8):
                            bk = pb[3] if m % 2 == 0 else pb[7]
                            for k in range(KT_D):
                                p.mm(bk[:, 0:512],
                                     cw1_sb[:, k, m * 128:(m + 1) * 128], fwb[:, k, :],
                                     start=(k == 0), stop=(k == KT_D - 1))
                            p.act(h1b[:, m, :], bk[:, 0:512], AF.Gelu,
                                  bias=cb1_sb[:, m:m + 1])
                        for m in range(4):
                            bk = pb[3] if m % 2 == 0 else pb[7]
                            for k in range(KT_D):
                                p.mm(bk[:, 0:512],
                                     cw2_sb[:, k, m * 128:(m + 1) * 128], h1b[:, k, :],
                                     start=(k == 0), stop=(k == KT_D - 1))
                            p.act(h2b[:, m, :], bk[:, 0:512], AF.Gelu,
                                  bias=cb2_sb[:, m:m + 1])
                        for m in range(2):
                            bk = pb[3] if m % 2 == 0 else pb[7]
                            for k in range(4):
                                p.mm(bk[:, 0:512],
                                     cw3_sb[:, k, m * 128:(m + 1) * 128], h2b[:, k, :],
                                     start=(k == 0), stop=(k == 3))
                            p.act(h3b[:, m, :], bk[:, 0:512], AF.Gelu,
                                  bias=cb3_sb[:, m:m + 1])
                        for k in range(2):
                            p.mm(pb[3][0:1, 0:512], cw4_sb[:, k, 0:1], h3b[:, k, :],
                                 start=(k == 0), stop=(k == 1))
                        p.act(tfr, pb[3][0:1, 0:512], AF.Sigmoid,
                              bias=cb4_sb[0:1, 0:1])
                        p.ts(tfr, tfr, 0.3, None, AL.mult)
                        p.tt(tfr, tfr,
                             gf_sb[0:1, 1:2].broadcast_to([1, 512]), AL.add)
                        p.ts(tfr, tfr, ar_sb[0:1, 1:2],
                             ar_sb[0:1, 2:3], AL.mult, AL.add)
                        p.dma(hb_cp.unsqueeze(0), tfr)
                        p.dma(cpp[:], hb_cp.unsqueeze(0).broadcast_to([128, 512]))
                        for i in range(KT_D):
                            op1 = AL.add if tsgn[i] > 0 else AL.subtract
                            p.tt(tmpt[:], fwc[:, tsrc[i], :], cpp[:], AL.mult)
                            p.tt(tmpt[:], fwc[:, i, :], tmpt[:], op1)
                            p.dma(fw[i * 128:(i + 1) * 128, nsl], tmpt[:], tag="fwst2")

                for _cyc in range(3):
                    for _nn in range(4):
                        slices.append((lambda c=_cyc, n=_nn: slice_body(c, n)))
                return slices

        if upto == "ln":
            layer_norm(xT, write_fw=True)
            with contextlib.ExitStack() as ph:
                tl = ph.enter_context(nc.sbuf_tensor("cpln", [128, T], F32))
                for i in range(KT_D):
                    p.dma(tl[:], fw[i * 128:(i + 1) * 128, :])
                    p.ts(tl[:], tl[:], 1.0, None, AL.mult)
                    p.dma(out_hbm[i * 128:(i + 1) * 128, :], tl[:])
                p.fence()
            stats = p.emit(st)
            return nc, stats
        layer_norm(xT, write_fw=True)
        mamba()
        if upto == "l0attn":
            pass
        else:
            ffn0()
        if upto in ("l0attn", "l0"):
            with contextlib.ExitStack() as ph:
                tl = ph.enter_context(nc.sbuf_tensor("cp_t", [128, T], F32))
                for i in range(KT_D):
                    p.dma(tl[:], x1[i * 128:(i + 1) * 128, :])
                    p.ts(tl[:], tl[:], 1.0, None, AL.mult)
                    p.dma(out_hbm[i * 128:(i + 1) * 128, :], tl[:])
                p.fence()
        else:
            layer_norm(x1, write_fw=True)
            if upto in ("full", "l1attn"):
                gdn(with_ffn1=(upto == "full"))
            if upto == "l1ffn":
                with contextlib.ExitStack() as ph:
                    tl = ph.enter_context(nc.sbuf_tensor("cpx1", [128, T], F32))
                    for i in range(KT_D):
                        p.dma(tl[:], x1[i * 128:(i + 1) * 128, :])
                        p.ts(tl[:], tl[:], 1.0, None, AL.mult)
                        p.dma(out_hbm[i * 128:(i + 1) * 128, :], tl[:])
                    p.fence()
            if upto == "l1ffn":
                with contextlib.ExitStack() as phX:
                    def arX(name, shape, dt=F32):
                        return phX.enter_context(nc.sbuf_tensor(_mk(name), list(shape), dt))
                    sgtX = arX("x_sgt", [128, 2 * T])
                    qkX = arX("x_qk", [128, T], BF16)
                    vX = arX("x_v", [128, T], BF16)
                    for sl in ffn1_make(arX, sgtX, qkX, vX):
                        sl()
                    p.fence()
                with contextlib.ExitStack() as phY:
                    tl2 = phY.enter_context(nc.sbuf_tensor("cpf", [128, T], F32))
                    tl3 = phY.enter_context(nc.sbuf_tensor("cpf2", [128, T], F32))
                    for i in range(KT_D):
                        p.dma(tl2[:], fw[i * 128:(i + 1) * 128, :])
                        p.dma(tl3[:], out_hbm[i * 128:(i + 1) * 128, :])
                        p.tt(tl3[:], tl3[:], tl2[:], AL.add)
                        p.dma(out_hbm[i * 128:(i + 1) * 128, :], tl3[:])
                    p.fence()
        stats = p.emit(st)
    return nc, stats



_CACHE = {}


def _consts_host():
    r = np.arange(128, dtype=np.float32)
    part = r[:, None]
    free = r[None, :]
    big = np.float32(-1e30)
    mb_ls = np.where(free < part, np.float32(0), big).astype(np.float32)
    mb_us = np.where(free > part, np.float32(0), big).astype(np.float32)
    mb_ui = np.where(free >= part, np.float32(0), big).astype(np.float32)
    creset = np.ones((16, T), dtype=np.float32)
    creset[:, ::GL] = 0.0
    return {
        "iden_c": np.eye(128, dtype=np.float32),
        "cvec": np.array(CONSTS, dtype=np.float32),
        "mb_ls": mb_ls, "mb_us": mb_us, "mb_ui": mb_ui,
        "creset": creset,
    }


def prep_inputs(inputs, b):
    f = np.float32
    tr = lambda a: np.ascontiguousarray(np.asarray(a, f).T)
    cc = lambda a: np.ascontiguousarray(np.asarray(a, f))
    d = {
        "xT": tr(inputs["L0_attn_x"][b]),
        "w_inT": tr(inputs["L0_attn_W_in"]),
        "convw": cc(np.asarray(inputs["L0_attn_conv_w"], f)[:, 0, :].reshape(NT_DI, 128, 3).transpose(1, 0, 2)),
        "convb": cc(np.asarray(inputs["L0_attn_conv_b"], f).reshape(NT_DI, 128).T),
        "alog": cc(np.asarray(inputs["L0_attn_A_log"], f).reshape(NT_DI, 128, NST).transpose(1, 0, 2)),
        "dsk": cc(np.asarray(inputs["L0_attn_D_skip"], f).reshape(NT_DI, 128).T),
        "wxT": cc(np.concatenate([np.asarray(inputs["L0_attn_W_xproj"], f)[1:65],
                                  np.asarray(inputs["L0_attn_W_xproj"], f)[0:1]], 0)
                  .T.reshape(NT_DI, 128, 65).transpose(1, 0, 2)),
        "wdt": cc(np.asarray(inputs["L0_attn_W_dt"], f)[:, 0].reshape(NT_DI, 128).T),
        "bdt": cc(np.asarray(inputs["L0_attn_b_dt"], f).reshape(NT_DI, 128).T),
        "alpha": cc(inputs["L0_attn_alpha"]).reshape(1),
        "woutT": tr(inputs["L0_attn_W_out"]),
        "wg": cc(inputs["L0_ffn_w_gate"]),
        "wu": cc(inputs["L0_ffn_w_up"]),
        "wd": cc(inputs["L0_ffn_w_down"]),
        "wqT": tr(inputs["L1_attn_Wq"]),
        "wkT": tr(inputs["L1_attn_Wk"]),
        "wvT": tr(inputs["L1_attn_Wv"]),
        "wbT": tr(inputs["L1_attn_Wb"]),
        "waT": tr(inputs["L1_attn_Wa"]),
        "al1": cc(inputs["L1_attn_A_log"]),
        "dtb1": cc(inputs["L1_attn_dt_bias"]),
        "woT": tr(inputs["L1_attn_Wo"]),
        "cq": cc(np.asarray(inputs["L1_attn_conv_q"], f).reshape(8, 128, 4).transpose(1, 0, 2)),
        "ck": cc(np.asarray(inputs["L1_attn_conv_k"], f).reshape(8, 128, 4).transpose(1, 0, 2)),
        "cv": cc(np.asarray(inputs["L1_attn_conv_v"], f).reshape(16, 128, 4).transpose(1, 0, 2)),
        "cw1": cc(inputs["L1_ffn_cn_w1"]),
        "cb1": cc(np.asarray(inputs["L1_ffn_cn_b1"], f).reshape(8, 128).T),
        "cw2": cc(inputs["L1_ffn_cn_w2"]),
        "cb2": cc(np.asarray(inputs["L1_ffn_cn_b2"], f).reshape(4, 128).T),
        "cw3": cc(inputs["L1_ffn_cn_w3"]),
        "cb3": cc(np.asarray(inputs["L1_ffn_cn_b3"], f).reshape(2, 128).T),
        "cw4": cc(inputs["L1_ffn_cn_w4"]),
        "cb4": cc(inputs["L1_ffn_cn_b4"]),
        "gw1": cc(inputs["L1_ffn_gc_w1"]),
        "gb1": cc(np.asarray(inputs["L1_ffn_gc_b1"], f).reshape(4, 128).T),
        "gw2": cc(inputs["L1_ffn_gc_w2"]),
        "gb2": cc(np.asarray(inputs["L1_ffn_gc_b2"], f).reshape(2, 128).T),
        "gw3": cc(inputs["L1_ffn_gc_w3"]),
        "gb3": cc(inputs["L1_ffn_gc_b3"]),
        "ar": cc(inputs["L1_ffn_adaptive_range"]).reshape(1),
    }
    d.update(_consts_host())
    return d


def kernel(upto="full", **inputs):
    if "prog" not in _CACHE or _CACHE.get("upto") != upto:
        nc, stats = build_model(upto)
        _CACHE["prog"] = nc
        _CACHE["upto"] = upto
    nc = _CACHE["prog"]
    in_maps = [prep_inputs(inputs, b) for b in range(8)]
    res = run_bass_kernel_spmd(nc, in_maps, core_ids=list(range(8)))
    outs = [np.ascontiguousarray(res.results[b]["out"].T) for b in range(8)]
    return np.stack(outs, axis=0)


# version 16
# speedup vs baseline: 1.0101x; 1.0101x over previous
import contextlib
import math

import numpy as np

import concourse.bass as bass
import concourse.mybir as mybir
from concourse.bass_utils import run_bass_kernel_spmd

F32 = mybir.dt.float32
BF16 = mybir.dt.bfloat16
AL = mybir.AluOpType
AF = mybir.ActivationFunctionType
AX = mybir.AxisListType
AP = bass.AP

ENGINES = ("tensor", "vector", "scalar", "gpsimd", "sync")


class _Op:
    __slots__ = ("engine", "method", "args", "kwargs", "reads", "writes", "idx",
                 "is_dma", "deps", "marked", "cnt", "dma_sem_key", "dma_seq", "tag")

    def __init__(self, engine, method, args, kwargs, reads, writes, idx, is_dma, tag):
        self.engine = engine
        self.method = method
        self.args = args
        self.kwargs = kwargs
        self.reads = reads
        self.writes = writes
        self.idx = idx
        self.is_dma = is_dma
        self.deps = []
        self.marked = False
        self.cnt = None
        self.dma_sem_key = None
        self.dma_seq = None
        self.tag = tag


def _ap_name(x):
    return x.name if isinstance(x, AP) else None


class Prog:

    def __init__(self, nc):
        self.nc = nc
        self.ops = []
        self.last_write = {}
        self.readers = {}
        self.cpool = None

    def _rec(self, engine, method, out_aps, in_aps, args, kwargs, is_dma=False, tag="",
             extra_deps=()):
        idx = len(self.ops)
        reads = [n for n in (_ap_name(a) for a in in_aps) if n]
        writes = [n for n in (_ap_name(a) for a in out_aps) if n]
        op = _Op(engine, method, args, kwargs, reads, writes, idx, is_dma, tag)
        deps = {}
        def _add(i, kind):
            if i == idx:
                return
            prev = deps.get(i)
            rank = {"war": 0, "waw": 1, "raw": 2}
            if prev is None or rank[kind] > rank[prev]:
                deps[i] = kind
        for d in extra_deps:
            _add(d, "raw")
        for b in reads:
            if b in self.last_write:
                _add(self.last_write[b], "raw")
        for b in writes:
            if b in self.last_write:
                _add(self.last_write[b], "waw")
            for r in self.readers.get(b, ()):
                _add(r, "war")
        op.deps = sorted(deps.items())
        for b in reads:
            self.readers.setdefault(b, []).append(idx)
        for b in writes:
            self.last_write[b] = idx
            self.readers[b] = []
        self.ops.append(op)
        return op

    def op(self, engine, method, outs, ins, *args, tag="", **kwargs):
        return self._rec(engine, method, outs, ins, args, kwargs, tag=tag)

    def mm(self, out, lhsT, rhs, start=True, stop=True, tile_position=None, tag=""):
        kw = dict(start=start, stop=stop)
        if tile_position is not None:
            kw["tile_position"] = tile_position
        return self._rec("tensor", "matmul", [out], [lhsT, rhs],
                         (out, lhsT, rhs), kw, tag=tag)

    def transpose(self, out, in_, ident, tag=""):
        return self._rec("tensor", "transpose", [out], [in_, ident],
                         (out, in_, ident), {}, tag=tag)

    def act(self, out, in_, func, bias=0.0, scale=1.0, accum_out=None, tag=""):
        ins = [in_] + [a for a in (bias, scale) if isinstance(a, AP)]
        if self.cpool is not None and any(
                isinstance(a, (float, int)) for a in (bias, scale)):
            ins.append(self.cpool)
        outs = [out] + ([accum_out] if accum_out is not None else [])
        kw = dict(bias=bias, scale=scale)
        if accum_out is not None:
            kw["accum_out"] = accum_out
        return self._rec("scalar", "activation", outs, ins, (out, in_, func), kw, tag=tag)

    def tt(self, out, in0, in1, alu, eng="vector", tag=""):
        return self._rec(eng, "tensor_tensor", [out], [in0, in1], (out, in0, in1, alu), {}, tag=tag)

    def ts(self, out, in0, s1, s2, op0, op1=AL.bypass, eng="vector", accum_out=None, tag=""):
        ins = [in0] + [a for a in (s1, s2) if isinstance(a, AP)]
        outs = [out] + ([accum_out] if accum_out is not None else [])
        kw = {}
        if accum_out is not None:
            kw["accum_out"] = accum_out
        return self._rec(eng, "tensor_scalar", outs, ins, (out, in0, s1, s2, op0, op1), kw, tag=tag)

    def stt(self, out, in0, s, in1, op0, op1, eng="vector", tag=""):
        ins = [in0, in1] + ([s] if isinstance(s, AP) else [])
        return self._rec(eng, "scalar_tensor_tensor", [out], ins, (out, in0, s, in1, op0, op1), {}, tag=tag)

    def scan(self, out, data0, data1, initial, op0, op1, tag=""):
        ins = [data0, data1] + ([initial] if isinstance(initial, AP) else [])
        return self._rec("vector", "tensor_tensor_scan", [out], ins,
                         (out, data0, data1, initial, op0, op1), {}, tag=tag)

    def red(self, out, in_, axis, alu, absval=None, negate=None, tag=""):
        kw = {}
        if absval is not None:
            kw["apply_absolute_value"] = absval
        if negate is not None:
            kw["negate"] = negate
        return self._rec("vector", "tensor_reduce", [out], [in_], (out, in_, axis, alu), kw, tag=tag)

    def copy(self, out, in_, eng="vector", tag=""):
        meth = "copy" if eng == "scalar" else "tensor_copy"
        return self._rec(eng, meth, [out], [in_], (out, in_), {}, tag=tag)

    def recip(self, out, in_, tag=""):
        return self._rec("vector", "reciprocal", [out], [in_], (out, in_), {}, tag=tag)

    def shuffle(self, out, in_, mask, tag=""):
        return self._rec("vector", "stream_shuffle", [out], [in_], (out, in_, mask), {}, tag=tag)

    def memset(self, ap, val, eng="vector", tag=""):
        return self._rec(eng, "memset", [ap], [], (ap, val), {}, tag=tag)

    def fence(self):
        last = {}
        last_dma = {}
        for op in self.ops:
            last[op.engine] = op.idx
            if op.is_dma:
                last_dma[op.dma_sem_key] = op.idx
        dma_deps = list(last_dma.values())
        for e in ENGINES:
            deps = [v for k, v in last.items() if k != e] + dma_deps
            self._rec(e, "nop", [], [], (), {}, tag="fence", extra_deps=deps)

    def dma(self, out, in_, eng="sync", tag="", slow_ok=False):
        kw = {"allow_slow_non_contiguous": True} if slow_ok else {}
        op = self._rec(eng, "dma_start", [out], [in_], (out, in_), kw, is_dma=True, tag=tag)
        op.dma_sem_key = op.writes[0] if op.writes else ("dmar_" + (op.reads[0] if op.reads else "x"))
        return op

    def dma_transpose(self, out, in_, eng="sync", tag=""):
        op = self._rec(eng, "dma_start_transpose", [out], [in_], (out, in_), {}, is_dma=True, tag=tag)
        op.dma_sem_key = op.writes[0] if op.writes else ("dmar_" + (op.reads[0] if op.reads else "x"))
        return op

    def emit(self, stack):
        nc = self.nc
        dma_count = {}
        for op in self.ops:
            if op.is_dma:
                k = op.dma_sem_key
                dma_count[k] = dma_count.get(k, 0) + 1
                op.dma_seq = dma_count[k]
        SE_SYNC = ("vector", "scalar", "gpsimd")

        def _needs_sync(dep, op, kind):
            if dep.is_dma:
                return False
            if dep.engine != op.engine:
                return True
            if op.is_dma:
                return kind in ("raw", "waw")
            return dep.engine in SE_SYNC

        for op in self.ops:
            for d, kind in op.deps:
                dep = self.ops[d]
                if _needs_sync(dep, op, kind):
                    dep.marked = True
        cnt = dict.fromkeys(ENGINES, 0)
        for op in self.ops:
            if op.marked:
                cnt[op.engine] += 1
                op.cnt = cnt[op.engine]
        eng_sem = {e: stack.enter_context(self.nc.semaphore(f"es_{e}"))
                   for e in ENGINES if cnt[e] > 0}
        dma_sems = {k: stack.enter_context(self.nc.semaphore(f"ds{i}"))
                    for i, k in enumerate(dma_count)}
        watermark = {}
        all_waits = []
        for op in self.ops:
            waits = {}
            for d, kind in op.deps:
                dep = self.ops[d]
                if dep.is_dma:
                    sem = dma_sems[dep.dma_sem_key]
                    v = 16 * dep.dma_seq
                elif not _needs_sync(dep, op, kind):
                    continue
                else:
                    sem = eng_sem[dep.engine]
                    v = dep.cnt
                key = sem.name
                if waits.get(key, (None, -1))[1] < v:
                    waits[key] = (sem, v)
            pruned = []
            for key, (sem, v) in waits.items():
                wm = watermark.get((op.engine, key), -1)
                if v > wm:
                    pruned.append((sem, v))
                    watermark[(op.engine, key)] = v
            all_waits.append(pruned)

        final_waits = [(dma_sems[k], 16 * dma_count[k]) for k in dma_count]

        stats = {e: 0 for e in ENGINES}
        for op in self.ops:
            stats[op.engine] += 1

        def emit_engine(eng_name):
            eng = getattr(nc, eng_name)
            my_sem = eng_sem.get(eng_name)
            for op, waits in zip(self.ops, all_waits):
                if op.engine != eng_name:
                    continue
                for sem, v in waits:
                    eng.wait_ge(sem, v)
                try:
                    inst = getattr(eng, op.method)(*op.args, **op.kwargs)
                except Exception:
                    print("EMIT FAIL", op.engine, op.method, op.tag,
                          [type(a).__name__ for a in op.args])
                    raise
                if op.is_dma:
                    inst.then_inc(dma_sems[op.dma_sem_key], 16)
                elif op.marked:
                    inst.then_inc(my_sem, 1)
            if eng_name == "sync":
                for sem, v in final_waits:
                    eng.wait_ge(sem, v)

        with nc.Block() as blk:
            for name, dec in (("sync", blk.sync), ("tensor", blk.tensor),
                              ("vector", blk.vector), ("scalar", blk.scalar),
                              ("gpsimd", blk.gpsimd)):
                def body(_eng, _name=name):
                    emit_engine(_name)
                dec(body)
        return stats



T = 2048
D = 1024
DI = 2048
NST = 32
KT_D = 8
NT_DI = 16
TCH = 256
NCH = T // TCH
FI = 2816
NIT = FI // 128
GH, GDK, GDV = 16, 64, 128
GL = 128
GC = T // GL

CONSTS = [0.5, -0.5, -1.0, 2.0, 1.0 + 1e-6, 1e-5, 1e-6, 1.0 / 128.0, 0.1,
          1.0, 0.0]

QSTEPS = [(0.25, 0.5), (0.75, 0.5), (1.25, 0.5), (1.75, 0.5),
          (2.5, 1.0), (3.5, 1.0), (5.0, 2.0)]


def build_model(upto="full", dbg=()):
    nc = bass.Bass("TRN2", target_bir_lowering=False, debug=False)
    p = Prog(nc)

    def din(name, shape):
        return nc.dram_tensor(name, list(shape), F32, kind="ExternalInput").ap()

    def dscratch(name, shape, dt=F32):
        return nc.dram_tensor(name, list(shape), dt, kind="Internal").ap()

    xT = din("xT", (D, T))
    w_inT = din("w_inT", (D, 2 * DI))
    convw = din("convw", (128, NT_DI, 3))
    convb = din("convb", (128, NT_DI))
    alog = din("alog", (128, NT_DI, NST))
    dskin = din("dsk", (128, NT_DI))
    wxT = din("wxT", (128, NT_DI, 65))
    wdt = din("wdt", (128, NT_DI))
    bdt = din("bdt", (128, NT_DI))
    alpha_in = din("alpha", (1,))
    woutT = din("woutT", (DI, D))
    wg_in = din("wg", (FI, D))
    wu_in = din("wu", (FI, D))
    wd_in = din("wd", (D, FI))
    wqT = din("wqT", (D, D))
    wkT = din("wkT", (D, D))
    wvT = din("wvT", (D, 2048))
    wbT = din("wbT", (D, 16))
    waT = din("waT", (D, 16))
    al1 = din("al1", (16,))
    dtb1 = din("dtb1", (16,))
    woT = din("woT", (2048, D))
    cq_in = din("cq", (128, 8, 4))
    ck_in = din("ck", (128, 8, 4))
    cv_in = din("cv", (128, 16, 4))
    cw1 = din("cw1", (1024, 1024))
    cb1 = din("cb1", (128, 8))
    cw2 = din("cw2", (1024, 512))
    cb2 = din("cb2", (128, 4))
    cw3 = din("cw3", (512, 256))
    cb3 = din("cb3", (128, 2))
    cw4 = din("cw4", (256, 1))
    cb4 = din("cb4", (1,))
    gw1 = din("gw1", (1024, 512))
    gb1 = din("gb1", (128, 4))
    gw2 = din("gw2", (512, 256))
    gb2 = din("gb2", (128, 2))
    gw3 = din("gw3", (256, 1))
    gb3 = din("gb3", (1,))
    ar_in = din("ar", (1,))
    iden_in = din("iden_c", (128, 128))
    cvec = din("cvec", (len(CONSTS),))
    mb_ls = din("mb_ls", (128, 128))
    mb_us = din("mb_us", (128, 128))
    mb_ui = din("mb_ui", (128, 128))
    creset = din("creset", (16, T))

    out_hbm = nc.dram_tensor("out", [D, T], F32, kind="ExternalOutput").ap()
    dbgo = {}
    if dbg:
        for nm, shape, dt in [("d_u", (128, NT_DI, TCH), BF16), ("d_z", (128, NT_DI, TCH), BF16),
                              ("d_proj", (65, TCH), F32), ("d_dtr", (128, TCH), F32),
                              ("d_brep", (128, NST, TCH), BF16), ("d_crep", (128, NST, TCH), BF16),
                              ("d_dta", (128, NST, TCH + 1), BF16), ("d_x", (128, NST, TCH + 1), BF16),
                              ("d_h", (128, NST, TCH + 1), BF16), ("d_dt", (128, TCH), F32),
                              ("d_y", (128, TCH), F32), ("d_yg", (128, NT_DI, TCH), BF16)]:
            dbgo[nm] = nc.dram_tensor(nm, list(shape), dt, kind="ExternalOutput").ap()
    if dbg and "g" in dbg:
        for nm, shape, dt in [("g_qd", (128, 8, T), BF16), ("g_kd", (128, 8, T), BF16),
                              ("g_sg", (16, T), F32), ("g_ta", (16, T), F32),
                              ("g_P0", (128, 128), BF16), ("g_P0T", (128, 128), BF16),
                              ("g_Wd", (128, 128), BF16), ("g_obd", (128, 128), F32),
                              ("g_Sd", (128, 8, 128), BF16), ("g_vt", (128, 128), BF16),
                              ("g_ost", (128, 128), BF16)]:
            dbgo[nm] = nc.dram_tensor(nm, list(shape), dt, kind="ExternalOutput").ap()

    x1 = dscratch("x1", (D, T))
    fw = dscratch("fw", (D, T))
    h_hbm = dscratch("h_hbm", (T, FI), F32)
    xq_hbm = dscratch("xq_hbm", (16, 128, 8, 128), BF16)
    v_hbm = dscratch("v_hbm", (16, 128, T), BF16)
    q_hbm = dscratch("q_hbm", (8, 128, T), BF16)
    k_hbm = dscratch("k_hbm", (8, 128, T), BF16)
    o_hbm = dscratch("o_hbm", (2048, T), BF16)
    hb_dtr = dscratch("hb_dtr", (TCH,))
    hb_b = dscratch("hb_b", (NST * TCH,), BF16)
    hb_c = dscratch("hb_c", (NST * TCH,), BF16)
    hb_row1 = dscratch("hb_row1", (T,))
    hb_row2 = dscratch("hb_row2", (T,))
    hb_row3 = dscratch("hb_row3", (T,))
    hb_row4 = dscratch("hb_row4", (T,))
    hb_row5 = dscratch("hb_row5", (T,))
    hb_row6 = dscratch("hb_row6", (T,))
    hb_sc = dscratch("hb_sc", (8,))
    hb_se = dscratch("hb_se", (16 * 16,))
    sg_hbm = dscratch("sg_hbm", (2, 16, T))
    hb_nv = dscratch("hb_nv", (2 * T,))
    hb_cp = dscratch("hb_cp", (512,))

    _uid = [0]

    def _mk(n):
        _uid[0] += 1
        return f"{n}_{_uid[0]}"

    st = contextlib.ExitStack()
    with st:
        def sb(name, shape, dt=F32):
            return st.enter_context(nc.sbuf_tensor(name, list(shape), dt))

        cpool = sb("cpool", [128, len(CONSTS)])
        p.dma(cpool[:], cvec.unsqueeze(0).broadcast_to([128, len(CONSTS)]))
        for i, v in enumerate(CONSTS):
            nc.const_aps.register(F32, v, cpool[:, i:i + 1])
        p.cpool = cpool[:]
        ones_col = cpool[:, CONSTS.index(1.0):CONSTS.index(1.0) + 1]

        iden = sb("iden", [128, 128])
        iden_bf = sb("iden_bf", [128, 128], BF16)
        p.dma(iden[:], iden_in)
        p.copy(iden_bf[:], iden[:])

        xn = sb("xn", [128, KT_D, T], BF16)

        def tposef(out_psum, in_ap):
            p.transpose(out_psum, in_ap, iden[:])

        def tposeb(out_psum, in_ap):
            p.transpose(out_psum, in_ap, iden_bf[:])

        pb = [st.enter_context(nc.psum_tensor(f"pb{i}", [128, 512], F32))
              for i in range(8)]

        def pbb(i):
            return pb[i][:, 0:64].bitcast(BF16)

        convw_sb = sb("convw_sb", [128, NT_DI, 3])
        convb_sb = sb("convb_sb", [128, NT_DI])
        wdt_sb = sb("wdt_sb", [128, NT_DI])
        bdt_sb = sb("bdt_sb", [128, NT_DI])
        dsk_sb = sb("dsk_sb", [128, NT_DI])
        AR_sb = sb("AR_sb", [128, NT_DI, NST])
        wxT_sb = sb("wxT_sb", [128, NT_DI, 65], BF16)
        lna_bc = sb("lna_bc", [128, 1])
        l1ma_bc = sb("l1ma_bc", [128, 1])
        edge = sb("edge", [128, NT_DI])
        hcarry = sb("hcarry", [128, NT_DI, NST], BF16)

        p.dma(convw_sb[:], convw)
        p.dma(convb_sb[:], convb)
        p.dma(wdt_sb[:], wdt)
        p.dma(bdt_sb[:], bdt)
        p.dma(dsk_sb[:], dskin)
        p.dma(AR_sb[:], alog)
        p.act(AR_sb[:], AR_sb[:], AF.Exp)
        p.ts(AR_sb[:], AR_sb[:], -1.0, None, AL.mult)
        with nc.sbuf_tensor("wxf", [128, NT_DI, 65], F32) as wxf:
            p.dma(wxf[:], wxT)
            p.copy(wxT_sb[:], wxf[:])
        p.fence()
        asc = sb("asc", [1, 4])
        p.dma(asc[0:1, 0:1], alpha_in.unsqueeze(0))
        p.act(asc[0:1, 1:2], asc[0:1, 0:1], AF.Sigmoid)
        p.act(asc[0:1, 2:3], asc[0:1, 1:2], AF.Ln)
        p.act(asc[0:1, 3:4], asc[0:1, 1:2], AF.Ln, scale=-1.0, bias=1.0)
        p.dma(hb_sc[0:1], asc[0:1, 2:3])
        p.dma(hb_sc[1:2], asc[0:1, 3:4])
        p.dma(lna_bc[:], hb_sc[0:1].unsqueeze(0).broadcast_to([128, 1]))
        p.dma(l1ma_bc[:], hb_sc[1:2].unsqueeze(0).broadcast_to([128, 1]))

        C1P = 1.0 + 1e-6

        def layer_norm(src, write_fw):
            with contextlib.ExitStack() as ph:
                def ar(name, shape, dt=F32):
                    return ph.enter_context(nc.sbuf_tensor(_mk(name), list(shape), dt))
                xld = ar("ln_x", [128, T])
                x2t = ar("ln_x2", [128, T])
                srow = ar("ln_s", [1, T])
                qrow = ar("ln_q", [1, T])
                abc = ar("ln_abc", [128, T])
                bbc = ar("ln_bbc", [128, T])
                xnf = ar("ln_xnf", [128, T])
                for kt in range(KT_D):
                    p.dma(xld[:], src[kt * 128:(kt + 1) * 128, :], tag="lnx")
                    p.act(x2t[:], xld[:], AF.Square)
                    for nn in range(4):
                        sl = slice(nn * 512, (nn + 1) * 512)
                        p.mm(pb[nn][0:1, 0:512], ones_col, xld[:, sl],
                             start=(kt == 0), stop=(kt == KT_D - 1))
                        p.mm(pb[4 + nn][0:1, 0:512], ones_col, x2t[:, sl],
                             start=(kt == 0), stop=(kt == KT_D - 1))
                for nn in range(4):
                    sl = slice(nn * 512, (nn + 1) * 512)
                    p.copy(srow[0:1, sl], pb[nn][0:1, 0:512])
                    p.copy(qrow[0:1, sl], pb[4 + nn][0:1, 0:512], eng="scalar")
                p.ts(srow[0:1, :], srow[0:1, :], 1.0 / D, None, AL.mult)
                p.ts(qrow[0:1, :], qrow[0:1, :], 1.0 / D, None, AL.mult)
                mu2 = ar("ln_m2", [1, T])
                p.tt(mu2[0:1, :], srow[0:1, :], srow[0:1, :], AL.mult)
                p.tt(qrow[0:1, :], qrow[0:1, :], mu2[0:1, :], AL.subtract)
                p.act(qrow[0:1, :], qrow[0:1, :], AF.Ln, bias=1e-5)
                p.act(qrow[0:1, :], qrow[0:1, :], AF.Exp, scale=-0.5)
                p.stt(mu2[0:1, :], srow[0:1, :], -1.0, qrow[0:1, :], AL.mult, AL.mult)
                p.dma(hb_row1.unsqueeze(0), qrow[0:1, :])
                p.dma(hb_row2.unsqueeze(0), mu2[0:1, :])
                p.dma(abc[:], hb_row1.unsqueeze(0).broadcast_to([128, T]))
                p.dma(bbc[:], hb_row2.unsqueeze(0).broadcast_to([128, T]))
                for kt in range(KT_D):
                    p.dma(xld[:], src[kt * 128:(kt + 1) * 128, :], tag="lnx2")
                    p.tt(xnf[:], xld[:], abc[:], AL.mult)
                    p.tt(xnf[:], xnf[:], bbc[:], AL.add)
                    p.copy(xn[:, kt, :], xnf[:])
                    if write_fw:
                        p.dma(fw[kt * 128:(kt + 1) * 128, :], xnf[:], tag="fwst")
            p.fence()

        def mamba():
            with contextlib.ExitStack() as ph:
                def ar(name, shape, dt=F32):
                    return ph.enter_context(nc.sbuf_tensor(_mk(name), list(shape), dt))
                dtraw_bc = ar("m_dtraw", [128, TCH])
                Brep = ar("m_brep", [128, NST, TCH], BF16)
                Crep = ar("m_crep", [128, NST, TCH], BF16)
                uch = ar("m_uch", [128, NT_DI, TCH], BF16)
                zch = ar("m_zch", [128, NT_DI, TCH], BF16)
                ygch = ar("m_ygch", [128, NT_DI, TCH], BF16)
                dtaR = [ar("m_dta", [128, NST, TCH + 1], BF16) for _ in range(2)]
                XbR = [ar("m_xb", [128, NST, TCH + 1], BF16) for _ in range(2)]
                hbufR = [ar("m_hbuf", [128, NST, TCH + 1], BF16) for _ in range(2)]
                x_inR = [ar("m_xin", [128, TCH + 2])] * 2
                dt_tR = [ar("m_dt", [128, TCH]) for _ in range(2)]
                dt_sR = [ar("m_dts", [128, TCH])] * 2
                du_tR = [ar("m_du", [128, TCH], BF16) for _ in range(2)]
                y_tR = [ar("m_yt", [128, TCH])] * 2
                proj_sb = ar("m_proj", [128, TCH])

                wstgbR = [ar("m_wstgb", [128, 16, 128], BF16) for _ in range(2)]
                xsbR = [ar("m_xsb", [128, TCH])] * 2
                Bbf = xsbR[0][:].bitcast(BF16)[:, 0:TCH]
                p.memset(dtaR[0][:, :, 0:1], 0.0)
                p.memset(dtaR[1][:, :, 0:1], 0.0)
                p.memset(edge[:], 0.0)

                tails = []

                def _finish_unit(dta, y_t, dtile):
                    hc = dta[:, :, 1:TCH + 1]
                    p.tt(hc[:, 0:16, :], hc[:, 0:16, :], hc[:, 16:32, :], AL.add)
                    p.tt(hc[:, 0:8, :], hc[:, 0:8, :], hc[:, 8:16, :], AL.add)
                    p.tt(hc[:, 0:4, :], hc[:, 0:4, :], hc[:, 4:8, :], AL.add)
                    p.tt(hc[:, 0:2, :], hc[:, 0:2, :], hc[:, 2:4, :], AL.add)
                    p.tt(y_t[:], hc[:, 0, :], hc[:, 1, :], AL.add)
                    p.stt(y_t[:], uch[:, dtile, :], dsk_sb[:, dtile:dtile + 1],
                          y_t[:], AL.mult, AL.add)
                    p.tt(ygch[:, dtile, :], y_t[:], zch[:, dtile, :], AL.mult)

                def phase_d(tcd):
                    t0d = tcd * TCH
                    for i in range(KT_D):
                        bank = pb[3 + (i % 2)]
                        wstgb = wstgbR[i % 2]
                        xsb = xsbR[i % 2]
                        p.dma(wstgb[:],
                              woutT.rearrange("(a p) c -> p a c", p=128)[:, :, i * 128:(i + 1) * 128],
                              tag="wout", eng="gpsimd")
                        for kt in range(NT_DI):
                            p.mm(bank[:, 0:TCH], wstgb[:, kt, :], ygch[:, kt, :],
                                 start=(kt == 0), stop=(kt == NT_DI - 1))
                        p.dma(xsb[:], xT[i * 128:(i + 1) * 128, t0d:t0d + TCH], tag="xres")
                        p.stt(xsb[:], bank[:, 0:TCH], 1.0, xsb[:], AL.mult, AL.add)
                        p.dma(x1[i * 128:(i + 1) * 128, t0d:t0d + TCH], xsb[:], tag="x1st")

                for tc in range(NCH):
                    t0 = tc * TCH
                    nx = min(TCH + 1, T - t0)
                    for dtile in range(NT_DI):
                        wstgb = wstgbR[dtile % 2]
                        x_in = x_inR[dtile % 2]
                        dt_s = dt_sR[dtile % 2]
                        c0 = dtile * 128
                        p.dma(wstgb[:, 0:8, :],
                              w_inT.rearrange("(a p) c -> p a c", p=128)[:, :, c0:c0 + 128],
                              tag="wina", eng="gpsimd")
                        for kt in range(KT_D):
                            p.mm(pb[0][:, 0:nx], wstgb[:, kt, :], xn[:, kt, t0:t0 + nx],
                                 start=(kt == 0), stop=(kt == KT_D - 1))
                        zc0 = DI + dtile * 128
                        p.dma(wstgb[:, 8:16, :],
                              w_inT.rearrange("(a p) c -> p a c", p=128)[:, :, zc0:zc0 + 128],
                              tag="winb", eng="gpsimd")
                        for kt in range(KT_D):
                            p.mm(pb[1][:, 0:TCH], wstgb[:, 8 + kt, :], xn[:, kt, t0:t0 + TCH],
                                 start=(kt == 0), stop=(kt == KT_D - 1))
                        p.copy(x_in[:, 0:1], edge[:, dtile:dtile + 1])
                        p.copy(x_in[:, 1:1 + nx], pb[0][:, 0:nx])
                        if nx < TCH + 1:
                            p.memset(x_in[:, 1 + nx:TCH + 2], 0.0)
                        p.copy(edge[:, dtile:dtile + 1], x_in[:, TCH:TCH + 1])
                        p.ts(dt_s[:], x_in[:, 0:TCH], convw_sb[:, dtile, 0:1], None, AL.mult)
                        p.stt(dt_s[:], x_in[:, 1:TCH + 1], convw_sb[:, dtile, 1:2], dt_s[:],
                              AL.mult, AL.add)
                        p.stt(dt_s[:], x_in[:, 2:TCH + 2], convw_sb[:, dtile, 2:3], dt_s[:],
                              AL.mult, AL.add)
                        p.act(uch[:, dtile, :], dt_s[:], AF.Silu,
                              bias=convb_sb[:, dtile:dtile + 1])
                        p.act(zch[:, dtile, :], pb[1][:, 0:TCH], AF.Silu)
                        p.mm(pb[2][0:65, 0:TCH], wxT_sb[:, dtile, :], uch[:, dtile, :],
                             start=(dtile == 0), stop=(dtile == NT_DI - 1))
                    if "a" in dbg and tc == 0:
                        p.dma(dbgo["d_u"], uch[:])
                        p.dma(dbgo["d_z"], zch[:])
                    p.copy(proj_sb[0:65, :], pb[2][0:65, 0:TCH])
                    p.dma(hb_dtr.unsqueeze(0), proj_sb[64:65, :])
                    p.dma(dtraw_bc[:], hb_dtr.unsqueeze(0).broadcast_to([128, TCH]))
                    p.copy(Bbf[0:32], proj_sb[0:32, :])
                    p.dma(hb_b.rearrange("(n t) -> n t", n=NST), Bbf[0:32])
                    p.dma(Brep[:], hb_b.rearrange("(n t) -> n t", n=NST)
                          .unsqueeze(0).broadcast_to([128, NST, TCH]))
                    p.copy(Bbf[32:64], proj_sb[32:64, :])
                    p.dma(hb_c.rearrange("(n t) -> n t", n=NST), Bbf[32:64])
                    p.dma(Crep[:], hb_c.rearrange("(n t) -> n t", n=NST)
                          .unsqueeze(0).broadcast_to([128, NST, TCH]))
                    if "b" in dbg and tc == 0:
                        p.dma(dbgo["d_proj"], proj_sb[0:65, :])
                        p.dma(dbgo["d_dtr"], dtraw_bc[:])
                        p.dma(dbgo["d_brep"], Brep[:])
                        p.dma(dbgo["d_crep"], Crep[:])
                    if tc > 0:
                        phase_d(tc - 1)
                    def dt_dta(dtile):
                        dt_t = dt_tR[dtile % 2]
                        dt_s = dt_sR[dtile % 2]
                        dta = dtaR[dtile % 2]
                        p.ts(dt_s[:], dtraw_bc[:], wdt_sb[:, dtile:dtile + 1],
                             bdt_sb[:, dtile:dtile + 1], AL.mult, AL.add)
                        p.ts(dt_s[:], dt_s[:], 80.0, None, AL.min)
                        p.act(dt_t[:], dt_s[:], AF.Exp)
                        p.act(dt_t[:], dt_t[:], AF.Ln, bias=1.0)
                        p.tt(dta[:, :, 1:TCH + 1],
                             dt_t[:].unsqueeze(1).broadcast_to([128, NST, TCH]),
                             AR_sb[:, dtile, :].unsqueeze(2).broadcast_to([128, NST, TCH]),
                             AL.mult)

                    dt_dta(0)
                    for dtile in range(NT_DI):
                        dta = dtaR[dtile % 2]
                        Xb = XbR[dtile % 2]
                        u1b = Xb[:, :, 1:TCH + 1]
                        hbuf = hbufR[dtile % 2]
                        dt_t = dt_tR[dtile % 2]
                        du_t = du_tR[dtile % 2]
                        y_t = y_tR[dtile % 2]
                        p.act(u1b, dta[:, :, 1:TCH + 1], AF.Ln, scale=-0.5, bias=C1P)
                        p.act(dta[:, :, 1:TCH + 1], dta[:, :, 1:TCH + 1], AF.Exp,
                              bias=lna_bc[:, 0:1])
                        p.act(u1b, u1b, AF.Exp, scale=-1.0, bias=l1ma_bc[:, 0:1])
                        if dtile + 1 < NT_DI:
                            dt_dta(dtile + 1)
                        p.tt(dta[:, :, 1:TCH + 1], dta[:, :, 1:TCH + 1], u1b, AL.add)
                        p.tt(du_t[:], dt_t[:], uch[:, dtile, :], AL.mult)
                        p.tt(Xb[:, :, 1:TCH + 1],
                             du_t[:].unsqueeze(1).broadcast_to([128, NST, TCH]),
                             Brep[:], AL.mult)
                        if tc == 0:
                            p.memset(Xb[:, :, 0:1], 0.0)
                        else:
                            p.copy(Xb[:, :, 0:1], hcarry[:, dtile, :].unsqueeze(2))
                        p.scan(hbuf[:].rearrange("p a b -> p (a b)"),
                               dta[:].rearrange("p a b -> p (a b)"),
                               Xb[:].rearrange("p a b -> p (a b)"),
                               0.0, AL.mult, AL.add)
                        if "c" in dbg and tc == 0 and dtile == 0:
                            p.dma(dbgo["d_dta"], dta[:])
                            p.dma(dbgo["d_x"], Xb[:])
                            p.dma(dbgo["d_h"], hbuf[:])
                            p.dma(dbgo["d_dt"], dt_t[:])
                        p.copy(hcarry[:, dtile, :], hbuf[:, :, TCH])
                        hc = dta[:, :, 1:TCH + 1]
                        p.tt(hc, hbuf[:, :, 1:TCH + 1], Crep[:], AL.mult)
                        _finish_unit(dta, y_t, dtile)
                    if "yg" in dbg and tc == 0:
                        p.dma(dbgo["d_yg"], ygch[:])
                phase_d(NCH - 1)
            p.fence()

        def quant_thunks(src, dst, W, qtmp):
            nb = W // 16
            mx, scl, inv, tq, ta, q, m = (qtmp[k] for k in
                                          ("mx", "scl", "inv", "tq", "ta", "q", "m"))
            th = []
            th.append(lambda: p.red(mx[:, 0:nb].unsqueeze(2),
                                    src.rearrange("p (a b) -> p a b", b=16),
                                    AX.X, AL.max, absval=True))
            th.append(lambda: p.ts(scl[:, 0:nb], mx[:, 0:nb], 1.0 / 6.0, 1e-12,
                                   AL.mult, AL.max))
            th.append(lambda: p.recip(inv[:, 0:nb], scl[:, 0:nb]))
            th.append(lambda: p.tt(tq[:, 0:W].rearrange("p (a b) -> p a b", b=16),
                                   src.rearrange("p (a b) -> p a b", b=16),
                                   inv[:, 0:nb].unsqueeze(2).broadcast_to([128, nb, 16]),
                                   AL.mult))
            th.append(lambda: p.act(ta[:, 0:W], tq[:, 0:W], AF.Abs))
            c0, w0 = QSTEPS[0]
            th.append(lambda: p.ts(q[:, 0:W], ta[:, 0:W], c0, w0, AL.is_gt, AL.mult))
            mbufs = [m[:, 0:W], m[:, W:2 * W]] if 2 * W <= FI else [m[:, 0:W]]
            for i, (c, w) in enumerate(QSTEPS[1:]):
                mb = mbufs[i % len(mbufs)]
                th.append(lambda c=c, w=w, mb=mb: p.ts(mb, ta[:, 0:W], c, w,
                                                       AL.is_gt, AL.mult))
                th.append(lambda mb=mb: p.tt(q[:, 0:W], q[:, 0:W], mb, AL.add))
            th.append(lambda: p.act(ta[:, 0:W], tq[:, 0:W], AF.Sign))
            th.append(lambda: p.tt(tq[:, 0:W], q[:, 0:W], ta[:, 0:W], AL.mult))
            th.append(lambda: p.tt(dst.rearrange("p (a b) -> p a b", b=16),
                                   tq[:, 0:W].rearrange("p (a b) -> p a b", b=16),
                                   scl[:, 0:nb].unsqueeze(2)
                                   .broadcast_to([128, nb, 16]), AL.mult))
            return th

        def quant(ph_ar, src, dst, W, qtmp):
            for t_ in quant_thunks(src, dst, W, qtmp):
                t_()

        def ffn0():
            with contextlib.ExitStack() as ph:
                def ar(name, shape, dt=F32):
                    return ph.enter_context(nc.sbuf_tensor(_mk(name), list(shape), dt))
                qtmp = {
                    "mx": ar("f_mx", [128, FI // 16]),
                    "scl": ar("f_scl", [128, FI // 16]),
                    "inv": ar("f_inv", [128, FI // 16]),
                    "tq": ar("f_tq", [128, FI]),
                    "ta": ar("f_ta", [128, FI]),
                    "q": ar("f_q", [128, FI], BF16),
                    "m": ar("f_m", [128, FI], BF16),
                }
                xqTR = [ar("f_xqT", [128, KT_D, 128], BF16) for _ in range(2)]
                wdqT = ar("f_wdqT", [128, NIT, 1024], BF16)
                xtm = ar("f_xtm", [128, D])
                xqtm = ar("f_xqtm", [128, D], BF16)
                wrow = ar("f_wrow", [128, FI])
                wql = ar("f_wql", [128, FI], BF16)
                wgqTcR = [ar("f_wgq", [128, KT_D, 512], BF16) for _ in range(2)]
                wuqTcR = [ar("f_wuq", [128, KT_D, 512], BF16) for _ in range(2)]
                tmpfR = [ar("f_tmpf", [128, 512]) for _ in range(2)]

                hchkR = [ar("f_hchk", [128, 512]) for _ in range(2)]
                htm = ar("f_htm", [128, FI])
                hqtm = ar("f_hqtm", [128, FI], BF16)
                hqTR = None
                ffntm = None
                ffndm = wrow[:, 0:D].rearrange("p (a c) -> p a c", a=KT_D)
                hqTR = [ar("f_hqT", [128, NIT, 128], BF16),
                        wgqTcR[0][:].rearrange("p a c -> p (a c)")[:, 0:NIT * 128]
                        .rearrange("p (a c) -> p a c", c=128)]
                x1c = wrow[:, D:2 * D].rearrange("p (a c) -> p a c", a=KT_D)
                ffntm = htm[:, 0:D]

                for t in range(16):
                    ts0 = t * 128
                    xqT = xqTR[t % 2]
                    p.dma(x1c[:], fw.rearrange("(a p) t -> p a t", p=128)[:, :, ts0:ts0 + 128],
                          tag="xnf32")
                    for k in range(KT_D):
                        tposef(pb[5][:, 0:128], x1c[:, k, :])
                        p.copy(xtm[:, k * 128:(k + 1) * 128], pb[5][:, 0:128], eng="scalar")
                    quant(ph, xtm[:], xqtm[:], D, qtmp)
                    for k in range(KT_D):
                        tposeb(pbb(5), xqtm[:, k * 128:(k + 1) * 128])
                        p.copy(xqT[:, k, :], pbb(5))
                    p.dma(xq_hbm[t, :, :, :], xqT[:], tag="xqst")
                for rt in range(KT_D):
                    p.dma(wrow[:], wd_in[rt * 128:(rt + 1) * 128, :], tag="wdld")
                    quant(ph, wrow[:], wql[:], FI, qtmp)
                    for It in range(NIT):
                        tposeb(pbb(5), wql[:, It * 128:(It + 1) * 128])
                        p.copy(wdqT[:, It, rt * 128:(rt + 1) * 128], pbb(5),
                               eng="scalar")
                ic_sizes = [512] * 5 + [256]

                def g_wprep_thunks(ic):
                    Iw = ic_sizes[ic]
                    i0 = ic * 512
                    wgqTc = wgqTcR[ic % 2]
                    wuqTc = wuqTcR[ic % 2]
                    th = []
                    for rt in range(Iw // 128):
                        r0 = i0 + rt * 128
                        th.append(lambda r0=r0: p.dma(wrow[:, 0:D],
                                                      wg_in[r0:r0 + 128, :], tag="wgld"))
                        th += quant_thunks(wrow[:, 0:D], wql[:, 0:D], D, qtmp)
                        for k in range(KT_D):
                            th.append(lambda k=k: tposeb(pbb(5),
                                                         wql[:, k * 128:(k + 1) * 128]))
                            th.append(lambda k=k, rt=rt, wgqTc=wgqTc:
                                      p.copy(wgqTc[:, k, rt * 128:(rt + 1) * 128],
                                             pbb(5)))
                        th.append(lambda r0=r0: p.dma(wrow[:, 0:D],
                                                      wu_in[r0:r0 + 128, :], tag="wuld"))
                        th += quant_thunks(wrow[:, 0:D], wql[:, 0:D], D, qtmp)
                        for k in range(KT_D):
                            th.append(lambda k=k: tposeb(pbb(5),
                                                         wql[:, k * 128:(k + 1) * 128]))
                            th.append(lambda k=k, rt=rt, wuqTc=wuqTc:
                                      p.copy(wuqTc[:, k, rt * 128:(rt + 1) * 128],
                                             pbb(5), eng="scalar"))
                    return th

                def g_mm_thunks(ic):
                    Iw = ic_sizes[ic]
                    i0 = ic * 512
                    wgqTc = wgqTcR[ic % 2]
                    wuqTc = wuqTcR[ic % 2]
                    th = []
                    for t in range(16):
                        xqT = xqTR[t % 2]
                        tmpf = tmpfR[t % 2]
                        hchk = hchkR[t % 2]
                        th.append(lambda t=t, xqT=xqT:
                                  p.dma(xqT[:], xq_hbm[t, :, :, :], tag="xqld"))
                        bg, bu = [(pb[0], pb[1]), (pb[2], pb[3]),
                                  (pb[6], pb[7])][t % 3]
                        for k in range(KT_D):
                            th.append(lambda k=k, bg=bg, xqT=xqT, wgqTc=wgqTc, Iw=Iw:
                                      p.mm(bg[:, 0:Iw], xqT[:, k, :], wgqTc[:, k, 0:Iw],
                                           start=(k == 0), stop=(k == KT_D - 1)))
                            th.append(lambda k=k, bu=bu, xqT=xqT, wuqTc=wuqTc, Iw=Iw:
                                      p.mm(bu[:, 0:Iw], xqT[:, k, :], wuqTc[:, k, 0:Iw],
                                           start=(k == 0), stop=(k == KT_D - 1)))
                        th.append(lambda bg=bg, tmpf=tmpf, Iw=Iw:
                                  p.act(tmpf[:, 0:Iw], bg[:, 0:Iw], AF.Silu))
                        th.append(lambda bu=bu, tmpf=tmpf, hchk=hchk, Iw=Iw:
                                  p.tt(hchk[:, 0:Iw], tmpf[:, 0:Iw], bu[:, 0:Iw],
                                       AL.mult))
                        th.append(lambda t=t, hchk=hchk, Iw=Iw, i0=i0:
                                  p.dma(h_hbm[t * 128:(t + 1) * 128, i0:i0 + Iw],
                                        hchk[:, 0:Iw], tag="hst"))
                    return th

                import itertools as _it
                for t_ in g_wprep_thunks(0):
                    t_()
                for ic in range(6):
                    mms = g_mm_thunks(ic)
                    wps = g_wprep_thunks(ic + 1) if ic + 1 < 6 else []
                    for a, b in _it.zip_longest(mms, wps):
                        if a:
                            a()
                        if b:
                            b()
                def h_prep_thunks(t):
                    hqT = hqTR[t % 2]
                    th = [lambda t=t: p.dma(htm[:], h_hbm[t * 128:(t + 1) * 128, :],
                                            tag="hld")]
                    th += quant_thunks(htm[:], hqtm[:], FI, qtmp)
                    for It in range(NIT):
                        th.append(lambda It=It: tposeb(pbb(5),
                                                       hqtm[:, It * 128:(It + 1) * 128]))
                        th.append(lambda It=It, hqT=hqT: p.copy(hqT[:, It, :], pbb(5)))
                    return th

                def h_mm_thunks(t):
                    ts0 = t * 128
                    hqT = hqTR[t % 2]
                    bg, bu = (pb[0], pb[1]) if t % 2 == 0 else (pb[2], pb[3])
                    th = []
                    for It in range(NIT):
                        th.append(lambda It=It, bg=bg, hqT=hqT:
                                  p.mm(bg[:, 0:512], hqT[:, It, :], wdqT[:, It, 0:512],
                                       start=(It == 0), stop=(It == NIT - 1)))
                        th.append(lambda It=It, bu=bu, hqT=hqT:
                                  p.mm(bu[:, 0:512], hqT[:, It, :], wdqT[:, It, 512:1024],
                                       start=(It == 0), stop=(It == NIT - 1)))
                    th.append(lambda bg=bg: p.copy(ffntm[:, 0:512], bg[:, 0:512]))
                    th.append(lambda bu=bu: p.copy(ffntm[:, 512:1024], bu[:, 0:512],
                                                   eng="scalar"))
                    for k in range(KT_D):
                        th.append(lambda k=k: tposef(pb[4][:, 0:128],
                                                     ffntm[:, k * 128:(k + 1) * 128]))
                        th.append(lambda k=k: p.copy(ffndm[:, k, :], pb[4][:, 0:128]))
                    th.append(lambda ts0=ts0:
                              p.dma(x1c[:], x1.rearrange("(a p) t -> p a t", p=128)
                                    [:, :, ts0:ts0 + 128], tag="x1ld"))
                    th.append(lambda: p.tt(x1c[:], x1c[:], ffndm[:], AL.add))
                    th.append(lambda ts0=ts0:
                              p.dma(x1.rearrange("(a p) t -> p a t", p=128)
                                    [:, :, ts0:ts0 + 128], x1c[:], tag="x1st2"))
                    return th

                for t_ in h_prep_thunks(0):
                    t_()
                for t in range(16):
                    mms = h_mm_thunks(t)
                    wps = h_prep_thunks(t + 1) if t + 1 < 16 else []
                    for a, b in _it.zip_longest(mms, wps):
                        if a:
                            a()
                        if b:
                            b()
            p.fence()

        def gdn(with_ffn1=True):
            with contextlib.ExitStack() as ph:
                def ar(name, shape, dt=F32):
                    return ph.enter_context(nc.sbuf_tensor(_mk(name), list(shape), dt))
                qk_ch = ar("g_qkch", [128, T], BF16)
                v_ch = ar("g_vch", [128, T], BF16)
                Kc_sbs = [None, None]
                Qc_sbs = [None, None]
                wstgb = ar("g_wstgb", [128, 8, 128], BF16)
                qlin = ar("g_qlin", [128, T + 3])
                cacc = ar("g_cacc", [128, T])
                sq_t0 = ar("g_sq0", [128, T])
                cwq_sb = ar("g_cwq", [128, 8, 4])
                cwk_sb = ar("g_cwk", [128, 8, 4])
                cwv_sb = ar("g_cwv", [128, 16, 4])
                sq_t = sq_t0[:, 0:T]
                wstg = sq_t0[:, 0:1024].rearrange("p (a c) -> p a c", a=8)
                nrm = ar("g_nrm", [2, T])
                nvb = qlin[:, 0:T]
                rowbuf = ar("g_rowbuf", [128, 2 * T])
                srow = rowbuf[0:16, 0:T]
                brow = rowbuf[0:16, T:2 * T]
                sgt = ar("g_sgt", [128, 2 * T])
                sig = sgt[0:16, 0:T]
                tau = sgt[0:16, T:2 * T]
                crst = ar("g_crst", [16, T], BF16)
                sp1 = cacc[0:16, 0:T]
                wab = ar("g_wab", [128, 8, 32], BF16)
                ealx = ar("g_eal", [16, 1])
                dtb_sb = ar("g_dtb", [16, 1])
                sigP = cacc[:, 0:T]
                sigT = ar("g_sigT", [128, GC, 16])
                nsigT = ar("g_nsigT", [128, GC, 16])
                tauT = ar("g_tauT", [128, GC, 16])
                seB = ar("g_seB", [128, GH, GC])
                gLcol = ar("g_gL", [128, GH, GC])
                sigBs = [qlin[:, 0:T], cacc[:, 0:T], sq_t[:, 0:T]]
                tauBs = [rowbuf[:, 0:T], rowbuf[:, T:2 * T],
                         ar("g_tau3", [128, T])[:]]
                glsAs = [ar("g_glsA", [128, GC]) for _ in range(3)]
                gam64s = [ar("g_gam64", [128, 128]) for _ in range(3)]
                mbls = ar("g_mbls", [128, 128], BF16)
                mbus = ar("g_mbus", [128, 128], BF16)
                mbui = ar("g_mbui", [128, 128], BF16)

                def prs(name, shape, dt=F32):
                    return [ar(name, shape, dt) for _ in range(3)]
                eargs = prs("g_earg", [128, 128])
                Emats = prs("g_E", [128, 128])
                EmatTs = prs("g_ET", [128, 128])

                P_sbs = prs("g_P", [128, 128], BF16)
                PT_sbs = prs("g_PT", [128, 128], BF16)
                P2_sbs = prs("g_P2", [128, 128], BF16)
                P2T_sbs = prs("g_P2T", [128, 128], BF16)
                W_sbs = prs("g_W", [128, 128], BF16)
                Vt_sbs = prs("g_Vt", [128, 128], BF16)
                Vc_sbs = prs("g_vc2", [128, 128], BF16)
                Kc_sbs = prs("g_kc", [128, 128], BF16)
                Qc_sbs = prs("g_qc", [128, 128], BF16)
                At_sbs = prs("g_At", [128, 128], BF16)
                Qg_sbs = prs("g_Qg", [128, 128], BF16)
                KpT_sbs = prs("g_KpT", [128, 64], BF16)
                dgl_sbs = prs("g_dgl", [128, 64], BF16)
                S_bfs = prs("g_Sbf", [128, 8, 128], BF16)
                o_sbs = prs("g_osb", [128, 128])
                rsins = prs("g_rsin", [128, 1])

                o_sts = prs("g_ost", [128, 128], BF16)

                e2 = ar("g_e2", [128, 2])

                p.dma(eargs[0][:], mb_ls)
                p.copy(mbls[:], eargs[0][:])
                p.dma(eargs[0][:], mb_us)
                p.copy(mbus[:], eargs[0][:])
                p.dma(eargs[0][:], mb_ui)
                p.copy(mbui[:], eargs[0][:])
                p.dma(crst[:], creset, eng="gpsimd")
                p.dma(cwq_sb[:], cq_in)
                p.dma(cwk_sb[:], ck_in)
                p.dma(cwv_sb[:], cv_in)

                def proj_conv(wT, ct, cw_sb, cwi, dst, dst_i):
                    p.dma(wstg, wT.rearrange("(a p) c -> p a c", p=128)
                          [:, :, ct * 128:(ct + 1) * 128], tag="gw")
                    p.copy(wstgb[:], wstg)
                    for nn in range(4):
                        bank = pb[nn % 2]
                        for kt in range(KT_D):
                            p.mm(bank[:, 0:512], wstgb[:, kt, :],
                                 xn[:, kt, nn * 512:(nn + 1) * 512],
                                 start=(kt == 0), stop=(kt == KT_D - 1))
                        p.copy(qlin[:, 3 + nn * 512:3 + (nn + 1) * 512], bank[:, 0:512])
                    p.memset(qlin[:, 0:3], 0.0)
                    p.ts(cacc[:], qlin[:, 0:T], cw_sb[:, cwi, 0:1], None, AL.mult)
                    for j in range(1, 4):
                        p.stt(cacc[:], qlin[:, j:j + T], cw_sb[:, cwi, j:j + 1],
                              cacc[:], AL.mult, AL.add)
                    p.act(dst[:, dst_i, :], cacc[:], AF.Silu)

                def qk_tile(wT, cw_sb, ct, dst_hbm):
                    proj_conv(wT, ct, cw_sb, ct, qk_ch[:].unsqueeze(1), 0)
                    p.act(sq_t[:], qk_ch[:], AF.Square)
                    for nn in range(4):
                        p.mm(pb[2][0:2, 0:512], e2[:], sq_t[:, nn * 512:(nn + 1) * 512],
                             start=True, stop=True)
                        p.copy(nrm[0:2, nn * 512:(nn + 1) * 512], pb[2][0:2, 0:512])
                    p.ts(nrm[:], nrm[:], 1e-12, None, AL.max)
                    p.act(nrm[:], nrm[:], AF.Ln)
                    p.act(nrm[:], nrm[:], AF.Exp, scale=-0.5)
                    p.dma(hb_nv.rearrange("(a t) -> a t", a=2), nrm[0:2, :])
                    p.dma(nvb, hb_nv.rearrange("(a t) -> a t", a=2)
                          .unsqueeze(1).broadcast_to([2, 64, T]))
                    p.tt(qk_ch[:], qk_ch[:], nvb, AL.mult)
                    p.dma(dst_hbm[ct, :, :], qk_ch[:], tag="qkst")

                p.memset(e2[:], 0.0)
                p.memset(e2[0:64, 0:1], 1.0)
                p.memset(e2[64:128, 1:2], 1.0)
                for ct in range(8):
                    qk_tile(wqT, cwq_sb, ct, q_hbm)
                for ct in range(8):
                    qk_tile(wkT, cwk_sb, ct, k_hbm)
                for ct in range(16):
                    proj_conv(wvT, ct, cwv_sb, ct, v_ch[:].unsqueeze(1), 0)
                    p.dma(v_hbm[ct, :, :], v_ch[:], tag="vst")


                if "g" in dbg:
                    p.dma(dbgo["g_qd"], q_dm[:])
                    p.dma(dbgo["g_kd"], k_dm[:])
                p.dma(wstg[:, :, 0:16], waT.rearrange("(a p) c -> p a c", p=128), tag="wa2")
                p.copy(wab[:, :, 0:16], wstg[:, :, 0:16])
                p.dma(wstg[:, :, 0:16], wbT.rearrange("(a p) c -> p a c", p=128), tag="wb2")
                p.copy(wab[:, :, 16:32], wstg[:, :, 0:16])
                for nn in range(4):
                    for kt in range(KT_D):
                        p.mm(pb[0][0:16, 0:512], wab[:, kt, 0:16],
                             xn[:, kt, nn * 512:(nn + 1) * 512],
                             start=(kt == 0), stop=(kt == KT_D - 1))
                        p.mm(pb[1][0:16, 0:512], wab[:, kt, 16:32],
                             xn[:, kt, nn * 512:(nn + 1) * 512],
                             start=(kt == 0), stop=(kt == KT_D - 1))
                    p.copy(srow[:, nn * 512:(nn + 1) * 512], pb[0][0:16, 0:512])
                    p.copy(brow[:, nn * 512:(nn + 1) * 512], pb[1][0:16, 0:512], eng="scalar")
                p.dma(dtb_sb[:], dtb1.unsqueeze(1), slow_ok=True)
                p.dma(ealx[:], al1.unsqueeze(1), slow_ok=True)
                p.act(ealx[:], ealx[:], AF.Exp)
                p.ts(ealx[:], ealx[:], -1.0, None, AL.mult)
                p.ts(srow[:], srow[:], dtb_sb[:, 0:1], 80.0, AL.add, AL.min)
                p.act(sp1, srow[:], AF.Exp)
                p.act(sp1, sp1, AF.Ln, bias=1.0)
                p.ts(srow[:], sp1, ealx[:, 0:1], None, AL.mult)
                p.act(brow[:], brow[:], AF.Sigmoid)
                p.act(brow[:], brow[:], AF.Ln)
                p.ts(brow[:], brow[:], 1.0, math.log(2.0), AL.mult, AL.add)
                p.scan(sig, crst[:], srow[:], 0.0, AL.mult, AL.add)
                p.tt(tau, sig, brow[:], AL.add)
                p.memset(sigP, 0.0)
                p.copy(sigP[0:16, :], sig)
                for c in range(GC):
                    tposef(pb[2][:, 0:128], sigP[:, c * 128:(c + 1) * 128])
                    p.copy(sigT[:, c, :], pb[2][:, 0:16])
                    p.ts(nsigT[:, c, :], pb[2][:, 0:16], -1.0, None, AL.mult)
                p.copy(sigP[0:16, :], tau)
                for c in range(GC):
                    tposef(pb[2][:, 0:128], sigP[:, c * 128:(c + 1) * 128])
                    p.copy(tauT[:, c, :], pb[2][:, 0:16])
                if "g" in dbg:
                    p.dma(dbgo["g_sg"], sig)
                    p.dma(dbgo["g_ta"], tau)
                p.dma(sg_hbm[0, :, :], sig)
                p.dma(sg_hbm[1, :, :], tau)
                p.copy(sigP[0:16, 0:GC], sig[:, GL - 1::GL])
                p.dma(hb_se.rearrange("(h c) -> h c", h=16), sigP[0:16, 0:GC])
                p.dma(seB[:], hb_se.rearrange("(h c) -> h c", h=16)
                      .unsqueeze(0).broadcast_to([128, GH, GC]))
                p.act(gLcol[:], seB[:], AF.Exp)
                p.memset(S_bfs[0][:], 0.0)
                p.memset(S_bfs[1][:], 0.0)
                p.memset(S_bfs[2][:], 0.0)

                hbrows_s = [(hb_row1, hb_row2), (hb_row3, hb_row4),
                            (hb_row5, hb_row6)]

                def head_prep(h):
                    st_ = h % 3
                    sigB, tauB = sigBs[st_], tauBs[st_]
                    r1, r2 = hbrows_s[st_]
                    p.dma(r1.unsqueeze(0), sg_hbm[0, h:h + 1, :])
                    p.dma(sigB, r1.unsqueeze(0).broadcast_to([128, T]))
                    p.dma(r2.unsqueeze(0), sg_hbm[1, h:h + 1, :])
                    p.dma(tauB, r2.unsqueeze(0).broadcast_to([128, T]))
                    p.tt(glsAs[st_][:], nsigT[:, :, h], seB[:, h, :], AL.add)
                    p.act(glsAs[st_][:], glsAs[st_][:], AF.Exp)

                def head_chunk(h, c):
                    ops = []
                    em = ops.append
                    st_ = h % 3
                    ct, hrow = h // 2, (h % 2) * 64
                    bG, bR = [(pb[0], pb[1]), (pb[4], pb[5]), (pb[2], pb[6])][st_]
                    bA = bG
                    bB = bR
                    bGb = bG[:, 0:64].bitcast(BF16)
                    bRb = bR[:, 0:64].bitcast(BF16)
                    sigB, tauB = sigBs[st_], tauBs[st_]
                    earg, Emat, EmatT = eargs[st_], Emats[st_], EmatTs[st_]
                    EmatA = Emat
                    P_sb, PT_sb, P2_sb, P2T_sb = (P_sbs[st_], PT_sbs[st_],
                                                  P2_sbs[st_], P2T_sbs[st_])
                    W_sb, Vt_sb, Vc_sb, At_sb = (W_sbs[st_], Vt_sbs[st_],
                                                 Vc_sbs[st_], At_sbs[st_])
                    KpT_sb, o_sb, rsin = KpT_sbs[st_], o_sbs[st_], rsins[st_]
                    o_n, o_st, gam64 = P2T_sbs[st_], o_sts[st_], gam64s[st_]
                    osq = eargs[st_]
                    glsA = glsAs[st_]
                    Qg_sb, dgl_sb, S_bf = Qg_sbs[st_], dgl_sbs[st_], S_bfs[st_]
                    csl = slice(c * 128, (c + 1) * 128)
                    Kc_b, Qc_b = Kc_sbs[st_], Qc_sbs[st_]
                    Kc = Kc_b[hrow:hrow + 64, :]
                    Qc = Qc_b[hrow:hrow + 64, :]
                    em(lambda: p.dma(Kc, k_hbm[ct, hrow:hrow + 64, csl], tag="kld"))
                    em(lambda: p.dma(Qc, q_hbm[ct, hrow:hrow + 64, csl], tag="qld"))
                    em(lambda: p.dma(Vc_sb[:], v_hbm[h, :, csl], tag="vld"))
                    S0b = S_bf[hrow:hrow + 64, h // 3, :]
                    em(lambda: p.mm(bG[:, 0:128], Kc, Kc, start=True, stop=True))
                    em(lambda: p.tt(earg[:], mbls[:], sigB[:, csl], AL.subtract))
                    em(lambda: p.act(Emat[:], earg[:], AF.Exp, bias=tauT[:, c, h:h + 1]))
                    em(lambda: p.tt(earg[:], mbus[:], tauB[:, csl], AL.add))
                    em(lambda: p.act(EmatT[:], earg[:], AF.Exp, bias=nsigT[:, c, h:h + 1]))
                    em(lambda: p.stt(P_sb[:], bG[:, 0:128], -1.0, Emat[:], AL.mult, AL.mult))
                    em(lambda: p.stt(PT_sb[:], bG[:, 0:128], -1.0, EmatT[:], AL.mult, AL.mult))
                    em(lambda: p.tt(earg[:], mbui[:], sigB[:, csl], AL.add))
                    em(lambda: p.act(EmatA[:], earg[:], AF.Exp, bias=nsigT[:, c, h:h + 1]))
                    em(lambda: p.transpose(bRb[:, 0:128], Vc_sb[:], iden_bf[:]))
                    em(lambda: p.copy(Vt_sb[:], bRb[:, 0:128]))
                    em(lambda: p.mm(bR[:, 0:128], Kc, S0b, start=True, stop=True))
                    em(lambda: p.act(rsin[:], tauT[:, c, h:h + 1], AF.Exp))
                    em(lambda: p.ts(rsin[:], rsin[:], -1.0, None, AL.mult))
                    em(lambda: p.stt(W_sb[:], bR[:, 0:128], rsin[:, 0:1], Vt_sb[:],
                                     AL.mult, AL.add))
                    pp = [(P_sb, PT_sb), (P2_sb, P2T_sb)]
                    for j in range(7):
                        Pa, PTa = pp[j % 2]
                        Pb, PTb = pp[1 - j % 2]
                        em(lambda PTa=PTa: p.mm(bA[:, 0:128], PTa[:], W_sb[:],
                                                start=True, stop=True))
                        em(lambda: p.tt(W_sb[:], W_sb[:], bA[:, 0:128], AL.add))
                        if j < 6:
                            em(lambda PTa=PTa, Pa=Pa: p.mm(bB[:, 0:128], PTa[:], Pa[:],
                                                           start=True, stop=True))
                            em(lambda Pb=Pb: p.copy(Pb[:], bB[:, 0:128]))
                            em(lambda Pa=Pa, PTa=PTa: p.mm(bB[:, 0:128], Pa[:], PTa[:],
                                                           start=True, stop=True))
                            em(lambda PTb=PTb: p.copy(PTb[:], bB[:, 0:128], eng="scalar"))
                    em(lambda: p.mm(bG[:, 0:128], Kc, Qc, start=True, stop=True))
                    em(lambda: p.stt(At_sb[:], bG[:, 0:128], 1.0, EmatA[:], AL.mult, AL.mult))
                    em(lambda: p.mm(bG[:, 0:128], At_sb[:], W_sb[:], start=True, stop=False))
                    em(lambda: p.act(gam64[hrow:hrow + 64, :], sigB[hrow:hrow + 64, csl],
                                     AF.Exp))
                    em(lambda: p.tt(Qg_sb[hrow:hrow + 64, :], Qc,
                                    gam64[hrow:hrow + 64, :], AL.mult))
                    em(lambda: p.mm(bG[:, 0:128], Qg_sb[hrow:hrow + 64, :], S0b,
                                    start=False, stop=True))
                    em(lambda: p.copy(o_sb[:], bG[:, 0:128]))
                    em(lambda: p.act(osq[:], o_sb[:], AF.Square, accum_out=rsin[:, 0:1]))
                    em(lambda: p.act(rsin[:], rsin[:], AF.Ln, scale=1.0 / 128.0, bias=1e-6))
                    em(lambda: p.act(rsin[:], rsin[:], AF.Exp, scale=-0.5))
                    em(lambda: p.ts(o_n[:], o_sb[:], rsin[:, 0:1], None, AL.mult))
                    em(lambda: p.transpose(bRb[:, 0:128], o_n[:], iden_bf[:]))
                    em(lambda: p.copy(o_st[:], bRb[:, 0:128]))
                    em(lambda: p.dma(o_hbm[h * 128:(h + 1) * 128, csl], o_st[:], tag="ost"))
                    em(lambda: p.transpose(bRb[:, 0:64], Kc,
                                           iden_bf[hrow:hrow + 64, hrow:hrow + 64]))
                    em(lambda: p.ts(KpT_sb[:, 0:64], bRb[:, 0:64], glsA[:, c:c + 1],
                                    None, AL.mult))
                    em(lambda: p.ts(dgl_sb[hrow:hrow + 64, :],
                                    iden_bf[hrow:hrow + 64, hrow:hrow + 64],
                                    gLcol[hrow:hrow + 64, h, c:c + 1], None, AL.mult))
                    em(lambda: p.mm(bR[hrow:hrow + 64, 0:128], KpT_sb[:, 0:64], W_sb[:],
                                    start=True, stop=False, tile_position=(0, hrow)))
                    em(lambda: p.mm(bR[hrow:hrow + 64, 0:128], dgl_sb[hrow:hrow + 64, :],
                                    S0b, start=False, stop=True,
                                    tile_position=(hrow, hrow)))
                    em(lambda: p.copy(S0b, bR[hrow:hrow + 64, 0:128]))
                    if "g" in dbg and h == 0 and c == 0:
                        em(lambda: p.dma(dbgo["g_P0"], P_sb[:]))
                        em(lambda: p.dma(dbgo["g_Wd"], W_sb[:]))
                        em(lambda: p.dma(dbgo["g_obd"], o_sb[:]))
                        em(lambda: p.dma(dbgo["g_Sd"], S_bf[:]))
                        em(lambda: p.dma(dbgo["g_ost"], o_st[:]))
                        em(lambda: p.dma(dbgo["g_P0T"], PT_sb[:]))
                        em(lambda: p.dma(dbgo["g_vt"], Vt_sb[:]))
                    return ops

                ffn_slices = (ffn1_make(ar, sgt, qk_ch, v_ch)
                              if with_ffn1 else [])
                import itertools
                nsl_total = len(ffn_slices)
                nrounds = (GH + 2) // 3
                for rnd in range(nrounds):
                    hs = [h for h in (3 * rnd, 3 * rnd + 1, 3 * rnd + 2) if h < GH]
                    for h in hs:
                        head_prep(h)
                    for c in range(GC):
                        tls = [head_chunk(h, c) for h in hs]
                        for grp in itertools.zip_longest(*tls):
                            for g in grp:
                                if g:
                                    g()
                    want = nsl_total * (rnd + 1) // nrounds
                    while ffn_slices and nsl_total - len(ffn_slices) < want:
                        ffn_slices.pop(0)()
                while ffn_slices:
                    ffn_slices.pop(0)()
            p.fence()

            with contextlib.ExitStack() as ph:
                def ar(name, shape, dt=F32):
                    return ph.enter_context(nc.sbuf_tensor(_mk(name), list(shape), dt))
                wstg2 = ar("g2_wstg", [128, 16, 1024])
                wstgb2 = ar("g2_wstgb", [128, 16, 1024], BF16)
                orh = ar("g2_orh", [128, 16, 512], BF16)
                outsb = ar("g2_outsb", [128, 512])
                fwsb = ar("g2_fwsb", [128, 512])
                p.dma(wstg2[:], woT.rearrange("(a p) c -> p a c", p=128), tag="wo")
                p.copy(wstgb2[:], wstg2[:])
                for nn in range(4):
                    nsl = slice(nn * 512, (nn + 1) * 512)
                    p.dma(orh[:], o_hbm.rearrange("(a p) t -> p a t", p=128)[:, :, nsl],
                          tag="old")
                    for i in range(KT_D):
                        bank = pb[i % 2]
                        for kt in range(16):
                            p.mm(bank[:, 0:512],
                                 wstgb2[:, kt, i * 128:(i + 1) * 128], orh[:, kt, :],
                                 start=(kt == 0), stop=(kt == 15))
                        p.dma(outsb[:], x1[i * 128:(i + 1) * 128, nsl], tag="x1ld2")
                        p.dma(fwsb[:], fw[i * 128:(i + 1) * 128, nsl], tag="fwld3")
                        p.stt(outsb[:], bank[:, 0:512], 1.0, outsb[:], AL.mult, AL.add)
                        p.tt(outsb[:], outsb[:], fwsb[:], AL.add)
                        p.dma(out_hbm[i * 128:(i + 1) * 128, nsl], outsb[:], tag="outst")
            p.fence()

        def ffn1_make(ar, sgt, qk_ch, v_ch):
            if True:
                gw1_sb = ar("l_gw1", [128, 8, 512], BF16)
                gw2_sb = ar("l_gw2", [128, 4, 256], BF16)
                gw3_sb = ar("l_gw3", [128, 2, 1], BF16)
                gb1_sb = ar("l_gb1", [128, 4])
                gb2_sb = ar("l_gb2", [128, 2])
                gb3_sb = ar("l_gb3", [1, 1])
                cw1_sb = ar("l_cw1", [128, 8, 1024], BF16)
                cw2_sb = ar("l_cw2", [128, 8, 512], BF16)
                cw3_sb = ar("l_cw3", [128, 4, 256], BF16)
                cw4_sb = ar("l_cw4", [128, 2, 1], BF16)
                cb1_sb = ar("l_cb1", [128, 8])
                cb2_sb = ar("l_cb2", [128, 4])
                cb3_sb = ar("l_cb3", [128, 2])
                cb4_sb = ar("l_cb4", [1, 1])
                ar_sb = ar("l_ar", [1, 4])
                gcv = ar("l_gcv", [128, 8])
                gcb = ar("l_gcb", [128, 8], BF16)
                gh1 = ar("l_gh1", [128, 4], BF16)
                gh2 = ar("l_gh2", [128, 2], BF16)
                gf_sb = ar("l_gf", [1, 2])
                fwc = sgt[:, 0:2 * T].rearrange("p (a c) -> p a c", a=8)[:, :, 0:512]
                fwb = ar("l_fwb", [128, 8, 512], BF16)
                h1b = ar("l_h1b", [128, 8, 512], BF16)
                h2b = qk_ch[:].rearrange("p (a c) -> p a c", a=4)
                h3b = v_ch[:, 0:1024].rearrange("p (a c) -> p a c", a=2)
                cpp = v_ch[:, 1024:2048].bitcast(F32)
                tfr = cpp[0:1, 0:512]
                tmpt = xn[:, 0, 0:1024].bitcast(F32)

                def ldw(dst_bf, src, kt_n, width):
                    p.dma(dst_bf[:, 0:kt_n, 0:width],
                          src.rearrange("(a p) c -> p a c", p=128), tag="lw",
                          slow_ok=(width < 16), eng="gpsimd")

                ldw(cw1_sb, cw1, 8, 1024)
                ldw(cw2_sb, cw2, 8, 512)
                ldw(cw3_sb, cw3, 4, 256)
                ldw(cw4_sb, cw4, 2, 1)
                ldw(gw1_sb, gw1, 8, 512)
                ldw(gw2_sb, gw2, 4, 256)
                ldw(gw3_sb, gw3, 2, 1)
                p.dma(cb1_sb[:], cb1)
                p.dma(cb2_sb[:, 0:4], cb2)
                p.dma(cb3_sb[:, 0:2], cb3)
                p.dma(cb4_sb[0:1, 0:1], cb4.unsqueeze(0))
                p.dma(gb1_sb[:], gb1)
                p.dma(gb2_sb[:, 0:2], gb2)
                p.dma(gb3_sb[0:1, 0:1], gb3.unsqueeze(0))
                p.dma(ar_sb[0:1, 0:1], ar_in.unsqueeze(0))
                p.ts(ar_sb[0:1, 1:2], ar_sb[0:1, 0:1], 2.0, None, AL.mult)
                p.ts(ar_sb[0:1, 2:3], ar_sb[0:1, 0:1], -1.0, 0.1, AL.mult, AL.add)

                for kt in range(KT_D):
                    p.red(gcv[:, kt:kt + 1], xn[:, kt, :], AX.X, AL.add)
                p.ts(gcv[:], gcv[:], 1.0 / T, None, AL.mult)
                p.copy(gcb[:], gcv[:])
                for m in range(4):
                    bk = pb[3] if m % 2 == 0 else pb[7]
                    for k in range(KT_D):
                        p.mm(bk[0:128, 0:1], gw1_sb[:, k, m * 128:(m + 1) * 128],
                             gcb[:, k:k + 1], start=(k == 0), stop=(k == KT_D - 1))
                    p.act(gh1[:, m:m + 1], bk[0:128, 0:1], AF.Gelu,
                          bias=gb1_sb[:, m:m + 1])
                for m in range(2):
                    bk = pb[3] if m % 2 == 0 else pb[7]
                    for k in range(4):
                        p.mm(bk[0:128, 0:1], gw2_sb[:, k, m * 128:(m + 1) * 128],
                             gh1[:, k:k + 1], start=(k == 0), stop=(k == 3))
                    p.act(gh2[:, m:m + 1], bk[0:128, 0:1], AF.Gelu,
                          bias=gb2_sb[:, m:m + 1])
                for k in range(2):
                    p.mm(pb[3][0:1, 0:1], gw3_sb[:, k, 0:1], gh2[:, k:k + 1],
                         start=(k == 0), stop=(k == 1))
                p.act(gf_sb[0:1, 0:1], pb[3][0:1, 0:1], AF.Sigmoid,
                      bias=gb3_sb[0:1, 0:1])
                p.ts(gf_sb[0:1, 1:2], gf_sb[0:1, 0:1], 0.7, None, AL.mult)

                tsrc = [4, 5, 6, 7, 0, 1, 2, 3]
                tsgn = [1.0, 1.0, -1.0, -1.0, -1.0, -1.0, 1.0, 1.0]
                slices = []

                def slice_body(cyc, nn):
                    if True:
                        nsl = slice(nn * 512, (nn + 1) * 512)
                        for kt in range(KT_D):
                            p.dma(fwc[:, kt, :], fw[kt * 128:(kt + 1) * 128, nsl],
                                  tag="fwld")
                        p.copy(fwb[:], fwc[:])
                        for m in range(8):
                            bk = pb[3] if m % 2 == 0 else pb[7]
                            for k in range(KT_D):
                                p.mm(bk[:, 0:512],
                                     cw1_sb[:, k, m * 128:(m + 1) * 128], fwb[:, k, :],
                                     start=(k == 0), stop=(k == KT_D - 1))
                            p.act(h1b[:, m, :], bk[:, 0:512], AF.Gelu,
                                  bias=cb1_sb[:, m:m + 1])
                        for m in range(4):
                            bk = pb[3] if m % 2 == 0 else pb[7]
                            for k in range(KT_D):
                                p.mm(bk[:, 0:512],
                                     cw2_sb[:, k, m * 128:(m + 1) * 128], h1b[:, k, :],
                                     start=(k == 0), stop=(k == KT_D - 1))
                            p.act(h2b[:, m, :], bk[:, 0:512], AF.Gelu,
                                  bias=cb2_sb[:, m:m + 1])
                        for m in range(2):
                            bk = pb[3] if m % 2 == 0 else pb[7]
                            for k in range(4):
                                p.mm(bk[:, 0:512],
                                     cw3_sb[:, k, m * 128:(m + 1) * 128], h2b[:, k, :],
                                     start=(k == 0), stop=(k == 3))
                            p.act(h3b[:, m, :], bk[:, 0:512], AF.Gelu,
                                  bias=cb3_sb[:, m:m + 1])
                        for k in range(2):
                            p.mm(pb[3][0:1, 0:512], cw4_sb[:, k, 0:1], h3b[:, k, :],
                                 start=(k == 0), stop=(k == 1))
                        p.act(tfr, pb[3][0:1, 0:512], AF.Sigmoid,
                              bias=cb4_sb[0:1, 0:1])
                        p.ts(tfr, tfr, 0.3, None, AL.mult)
                        p.tt(tfr, tfr,
                             gf_sb[0:1, 1:2].broadcast_to([1, 512]), AL.add)
                        p.ts(tfr, tfr, ar_sb[0:1, 1:2],
                             ar_sb[0:1, 2:3], AL.mult, AL.add)
                        p.dma(hb_cp.unsqueeze(0), tfr)
                        p.dma(cpp[:], hb_cp.unsqueeze(0).broadcast_to([128, 512]))
                        for i in range(KT_D):
                            op1 = AL.add if tsgn[i] > 0 else AL.subtract
                            p.tt(tmpt[:], fwc[:, tsrc[i], :], cpp[:], AL.mult)
                            p.tt(tmpt[:], fwc[:, i, :], tmpt[:], op1)
                            p.dma(fw[i * 128:(i + 1) * 128, nsl], tmpt[:], tag="fwst2")

                for _cyc in range(3):
                    for _nn in range(4):
                        slices.append((lambda c=_cyc, n=_nn: slice_body(c, n)))
                return slices

        if upto == "ln":
            layer_norm(xT, write_fw=True)
            with contextlib.ExitStack() as ph:
                tl = ph.enter_context(nc.sbuf_tensor("cpln", [128, T], F32))
                for i in range(KT_D):
                    p.dma(tl[:], fw[i * 128:(i + 1) * 128, :])
                    p.ts(tl[:], tl[:], 1.0, None, AL.mult)
                    p.dma(out_hbm[i * 128:(i + 1) * 128, :], tl[:])
                p.fence()
            stats = p.emit(st)
            return nc, stats
        layer_norm(xT, write_fw=True)
        mamba()
        if upto == "l0attn":
            pass
        else:
            ffn0()
        if upto in ("l0attn", "l0"):
            with contextlib.ExitStack() as ph:
                tl = ph.enter_context(nc.sbuf_tensor("cp_t", [128, T], F32))
                for i in range(KT_D):
                    p.dma(tl[:], x1[i * 128:(i + 1) * 128, :])
                    p.ts(tl[:], tl[:], 1.0, None, AL.mult)
                    p.dma(out_hbm[i * 128:(i + 1) * 128, :], tl[:])
                p.fence()
        else:
            layer_norm(x1, write_fw=True)
            if upto in ("full", "l1attn"):
                gdn(with_ffn1=(upto == "full"))
            if upto == "l1ffn":
                with contextlib.ExitStack() as ph:
                    tl = ph.enter_context(nc.sbuf_tensor("cpx1", [128, T], F32))
                    for i in range(KT_D):
                        p.dma(tl[:], x1[i * 128:(i + 1) * 128, :])
                        p.ts(tl[:], tl[:], 1.0, None, AL.mult)
                        p.dma(out_hbm[i * 128:(i + 1) * 128, :], tl[:])
                    p.fence()
            if upto == "l1ffn":
                with contextlib.ExitStack() as phX:
                    def arX(name, shape, dt=F32):
                        return phX.enter_context(nc.sbuf_tensor(_mk(name), list(shape), dt))
                    sgtX = arX("x_sgt", [128, 2 * T])
                    qkX = arX("x_qk", [128, T], BF16)
                    vX = arX("x_v", [128, T], BF16)
                    for sl in ffn1_make(arX, sgtX, qkX, vX):
                        sl()
                    p.fence()
                with contextlib.ExitStack() as phY:
                    tl2 = phY.enter_context(nc.sbuf_tensor("cpf", [128, T], F32))
                    tl3 = phY.enter_context(nc.sbuf_tensor("cpf2", [128, T], F32))
                    for i in range(KT_D):
                        p.dma(tl2[:], fw[i * 128:(i + 1) * 128, :])
                        p.dma(tl3[:], out_hbm[i * 128:(i + 1) * 128, :])
                        p.tt(tl3[:], tl3[:], tl2[:], AL.add)
                        p.dma(out_hbm[i * 128:(i + 1) * 128, :], tl3[:])
                    p.fence()
        stats = p.emit(st)
    return nc, stats



_CACHE = {}


def _consts_host():
    r = np.arange(128, dtype=np.float32)
    part = r[:, None]
    free = r[None, :]
    big = np.float32(-1e30)
    mb_ls = np.where(free < part, np.float32(0), big).astype(np.float32)
    mb_us = np.where(free > part, np.float32(0), big).astype(np.float32)
    mb_ui = np.where(free >= part, np.float32(0), big).astype(np.float32)
    creset = np.ones((16, T), dtype=np.float32)
    creset[:, ::GL] = 0.0
    return {
        "iden_c": np.eye(128, dtype=np.float32),
        "cvec": np.array(CONSTS, dtype=np.float32),
        "mb_ls": mb_ls, "mb_us": mb_us, "mb_ui": mb_ui,
        "creset": creset,
    }


def prep_inputs(inputs, b):
    f = np.float32
    tr = lambda a: np.ascontiguousarray(np.asarray(a, f).T)
    cc = lambda a: np.ascontiguousarray(np.asarray(a, f))
    d = {
        "xT": tr(inputs["L0_attn_x"][b]),
        "w_inT": tr(inputs["L0_attn_W_in"]),
        "convw": cc(np.asarray(inputs["L0_attn_conv_w"], f)[:, 0, :].reshape(NT_DI, 128, 3).transpose(1, 0, 2)),
        "convb": cc(np.asarray(inputs["L0_attn_conv_b"], f).reshape(NT_DI, 128).T),
        "alog": cc(np.asarray(inputs["L0_attn_A_log"], f).reshape(NT_DI, 128, NST).transpose(1, 0, 2)),
        "dsk": cc(np.asarray(inputs["L0_attn_D_skip"], f).reshape(NT_DI, 128).T),
        "wxT": cc(np.concatenate([np.asarray(inputs["L0_attn_W_xproj"], f)[1:65],
                                  np.asarray(inputs["L0_attn_W_xproj"], f)[0:1]], 0)
                  .T.reshape(NT_DI, 128, 65).transpose(1, 0, 2)),
        "wdt": cc(np.asarray(inputs["L0_attn_W_dt"], f)[:, 0].reshape(NT_DI, 128).T),
        "bdt": cc(np.asarray(inputs["L0_attn_b_dt"], f).reshape(NT_DI, 128).T),
        "alpha": cc(inputs["L0_attn_alpha"]).reshape(1),
        "woutT": tr(inputs["L0_attn_W_out"]),
        "wg": cc(inputs["L0_ffn_w_gate"]),
        "wu": cc(inputs["L0_ffn_w_up"]),
        "wd": cc(inputs["L0_ffn_w_down"]),
        "wqT": tr(inputs["L1_attn_Wq"]),
        "wkT": tr(inputs["L1_attn_Wk"]),
        "wvT": tr(inputs["L1_attn_Wv"]),
        "wbT": tr(inputs["L1_attn_Wb"]),
        "waT": tr(inputs["L1_attn_Wa"]),
        "al1": cc(inputs["L1_attn_A_log"]),
        "dtb1": cc(inputs["L1_attn_dt_bias"]),
        "woT": tr(inputs["L1_attn_Wo"]),
        "cq": cc(np.asarray(inputs["L1_attn_conv_q"], f).reshape(8, 128, 4).transpose(1, 0, 2)),
        "ck": cc(np.asarray(inputs["L1_attn_conv_k"], f).reshape(8, 128, 4).transpose(1, 0, 2)),
        "cv": cc(np.asarray(inputs["L1_attn_conv_v"], f).reshape(16, 128, 4).transpose(1, 0, 2)),
        "cw1": cc(inputs["L1_ffn_cn_w1"]),
        "cb1": cc(np.asarray(inputs["L1_ffn_cn_b1"], f).reshape(8, 128).T),
        "cw2": cc(inputs["L1_ffn_cn_w2"]),
        "cb2": cc(np.asarray(inputs["L1_ffn_cn_b2"], f).reshape(4, 128).T),
        "cw3": cc(inputs["L1_ffn_cn_w3"]),
        "cb3": cc(np.asarray(inputs["L1_ffn_cn_b3"], f).reshape(2, 128).T),
        "cw4": cc(inputs["L1_ffn_cn_w4"]),
        "cb4": cc(inputs["L1_ffn_cn_b4"]),
        "gw1": cc(inputs["L1_ffn_gc_w1"]),
        "gb1": cc(np.asarray(inputs["L1_ffn_gc_b1"], f).reshape(4, 128).T),
        "gw2": cc(inputs["L1_ffn_gc_w2"]),
        "gb2": cc(np.asarray(inputs["L1_ffn_gc_b2"], f).reshape(2, 128).T),
        "gw3": cc(inputs["L1_ffn_gc_w3"]),
        "gb3": cc(inputs["L1_ffn_gc_b3"]),
        "ar": cc(inputs["L1_ffn_adaptive_range"]).reshape(1),
    }
    d.update(_consts_host())
    return d


def kernel(upto="full", **inputs):
    if "prog" not in _CACHE or _CACHE.get("upto") != upto:
        nc, stats = build_model(upto)
        _CACHE["prog"] = nc
        _CACHE["upto"] = upto
    nc = _CACHE["prog"]
    in_maps = [prep_inputs(inputs, b) for b in range(8)]
    res = run_bass_kernel_spmd(nc, in_maps, core_ids=list(range(8)))
    outs = [np.ascontiguousarray(res.results[b]["out"].T) for b in range(8)]
    return np.stack(outs, axis=0)
